# version 20
# baseline (speedup 1.0000x reference)
import contextlib
import numpy as np
import ml_dtypes
import concourse.bass as bass
import concourse.mybir as mybir
from concourse.bass_utils import run_bass_kernel_spmd

F32 = mybir.dt.float32
BF16 = mybir.dt.bfloat16
AF = mybir.ActivationFunctionType
ALU = mybir.AluOpType
AX = mybir.AxisListType

S = 4096
C = 1024
NB = S // 128
NCH = 8
H = 16
HD = 64
K9 = 9
EPS = 1e-5
NCORES = 8


def build_nc(debug=False):
    nc = bass.Bass("TRN2", target_bir_lowering=False)

    xin = nc.dram_tensor("xin", [S, C], F32, kind="ExternalInput")
    wrT_h = nc.dram_tensor("wrT", [C, C], F32, kind="ExternalInput")
    wkT_h = nc.dram_tensor("wkT", [C, C], F32, kind="ExternalInput")
    wvT_h = nc.dram_tensor("wvT", [C, C], F32, kind="ExternalInput")
    cwq_h = nc.dram_tensor("cwq", [NCH, 128, 162], F32, kind="ExternalInput")
    mur_h = nc.dram_tensor("mur", [128, NCH], F32, kind="ExternalInput")
    muk_h = nc.dram_tensor("muk", [128, NCH], F32, kind="ExternalInput")
    idb_h = nc.dram_tensor("idbh", [128, 128], BF16, kind="ExternalInput")
    out_h = nc.dram_tensor("out", [S, C], F32, kind="ExternalOutput")

    kd = "ExternalOutput" if debug else "Internal"
    ln1tm = nc.dram_tensor("ln1tm", [S + 8, C], BF16, kind=kd)
    ln1fm = nc.dram_tensor("ln1fm", [C, S], BF16, kind=kd)
    qdr = nc.dram_tensor("qdr", [S + 8, 9 * 144], F32, kind=kd)
    wtd = nc.dram_tensor("wtd", [S, 144], BF16, kind=kd)
    x2d = nc.dram_tensor("x2d", [S, C], F32, kind=kd)
    ln3fm = nc.dram_tensor("ln3fm", [C, S + 1], BF16, kind=kd)
    xrd = nc.dram_tensor("xrd", [C, S], BF16, kind=kd)
    xkd = nc.dram_tensor("xkd", [C, S], BF16, kind=kd)
    tmpdbg = nc.dram_tensor("tmpdbg", [S, C], F32, kind=kd)
    ydbg = nc.dram_tensor("ydbg", [S, C], F32, kind=kd)
    svdbg = nc.dram_tensor("svdbg", [S, 20], F32, kind=kd)

    with contextlib.ExitStack() as st:
        def sb(name, shape, dt=F32):
            return st.enter_context(nc.sbuf_tensor(name, shape, dt))

        wr_sb = sb("wr_sb", [128, NCH * C], BF16)
        wk_sb = sb("wk_sb", [128, NCH * C], BF16)
        wv_sb = sb("wv_sb", [128, NCH * C], BF16)
        cwq_sb = sb("cwq_sb", [128, NCH * 162], BF16)
        mur_sb = sb("mur_sb", [128, NCH])
        muk_sb = sb("muk_sb", [128, NCH])
        idb = sb("idb", [128, 128], BF16)
        consts = sb("consts", [128, 4])

        xbuf = sb("xbuf", [128, 2 * C])
        lnb = sb("lnb", [128, 2 * C], BF16)
        fmst = sb("fmst", [128, 2 * C], BF16)
        junk = sb("junk", [128, C])
        sv = sb("sv", [128, 20])
        fmwin = sb("fmwin", [128, 2 * NCH * 128], BF16)
        qstage = sb("qstage", [128, 2 * 1296])
        acc = sb("acc", [128, 2 * 144])
        mx = sb("mx", [128, 2 * H])
        zsum = sb("zsum", [128, 2 * H])
        wtb = sb("wtb", [128, 2 * 144], BF16)
        xk9 = sb("xk9", [128, 9 * C], BF16)
        acc2 = sb("acc2", [128, C])
        ptmp = sb("ptmp", [128, C])
        acc2d = sb("acc2d", [128, C])
        tmp = sb("tmp", [128, 2 * C])
        x2b = sb("x2b", [128, 2 * C])
        fmc = sb("fmc", [128, 2 * (C + 1)], BF16)
        diff = sb("diff", [128, C], BF16)
        xrb = sb("xrb", [128, 2 * C], BF16)
        xkb = sb("xkb", [128, 2 * C], BF16)
        xr_sb = sb("xr_sb", [128, NCH * 512], BF16)
        xk_sb = sb("xk_sb", [128, NCH * 512], BF16)
        kact = sb("kact", [128, NCH * 512], BF16)
        kkb = sb("kkb", [128, 2 * 512])
        r_sb = sb("r_sb", [128, 4 * C], BF16)
        ffn = sb("ffn", [128, 4 * C])
        x2t = sb("x2t", [128, C])
        outb = sb("outb", [128, C])
        dum = sb("dum", [128, 4])

        psQa = st.enter_context(nc.psum_tensor("psQa", [128, 162], F32))
        psQb = st.enter_context(nc.psum_tensor("psQb", [128, 162], F32))
        psT0 = st.enter_context(nc.psum_tensor("psT0", [128, C], BF16))
        psT1 = st.enter_context(nc.psum_tensor("psT1", [128, C], BF16))
        psA = st.enter_context(nc.psum_tensor("psA", [128, 512], F32))
        psB = st.enter_context(nc.psum_tensor("psB", [128, 512], F32))
        psC = st.enter_context(nc.psum_tensor("psC", [128, 512], F32))
        psD = st.enter_context(nc.psum_tensor("psD", [128, 512], F32))

        dsm = st.enter_context(nc.semaphore("dsm"))
        gsm = st.enter_context(nc.semaphore("gsm"))
        vsm = st.enter_context(nc.semaphore("vsm"))
        ssm = st.enter_context(nc.semaphore("ssm"))
        tsm = st.enter_context(nc.semaphore("tsm"))
        psm = st.enter_context(nc.semaphore("psm"))

        nc.const_aps.register(F32, 0.0, consts[:, 0:1])
        nc.const_aps.register(F32, 1.0, consts[:, 1:2])
        nc.const_aps.register(F32, float(EPS), consts[:, 2:3])

        ct = dict(d=0, g=0, v=0, s=0, t=0, p=0)
        SEMS = dict(d=dsm, g=gsm, v=vsm, s=ssm, t=tsm, p=psm)
        EKEY = dict(vector="v", scalar="s", tensor="t", gpsimd="p")

        class Ops:
            def __init__(self):
                self.l = dict(sync=[], vector=[], scalar=[], tensor=[], gpsimd=[])

            def add(self, eng, fn):
                self.l[eng].append(fn)

            def emit(self):
                with nc.Block() as blk:
                    for eng, dec in (("sync", blk.sync), ("vector", blk.vector),
                                     ("scalar", blk.scalar), ("tensor", blk.tensor),
                                     ("gpsimd", blk.gpsimd)):
                        if self.l[eng]:
                            dec(lambda e, L=self.l[eng]: [f(e) for f in L])

        def dma(ops, dst, src, eng="sync", **kw):
            key = "g" if eng == "gpsimd" else "d"
            sem = SEMS[key]
            ct[key] += 16
            ops.add(eng, lambda e, a=dst, b=src, s2=sem, k2=kw:
                    e.dma_start(a, b, **k2).then_inc(s2, 16))
            return ct[key]

        def wait(ops, eng, sem_key, val):
            if val:
                ops.add(eng, lambda e, s2=SEMS[sem_key], v2=val: e.wait_ge(s2, v2))

        def ev(ops, eng, fn):
            key = EKEY[eng]
            sem = SEMS[key]
            ct[key] += 1
            ops.add(eng, lambda e, f=fn, s2=sem: f(e).then_inc(s2, 1))
            return ct[key]

        def op(ops, eng, fn):
            ops.add(eng, fn)

        def SX(p): return sv[:, 10 * p + 0: 10 * p + 1]
        def SQ(p): return sv[:, 10 * p + 1: 10 * p + 2]
        def MEAN(p): return sv[:, 10 * p + 2: 10 * p + 3]
        def NMEAN(p): return sv[:, 10 * p + 3: 10 * p + 4]
        def VAR(p): return sv[:, 10 * p + 4: 10 * p + 5]
        def RSTD(p): return sv[:, 10 * p + 5: 10 * p + 6]
        def NBIAS(p): return sv[:, 10 * p + 6: 10 * p + 7]
        def TMPV(p): return sv[:, 10 * p + 7: 10 * p + 8]
        def SXB(p): return sv[:, 10 * p + 8: 10 * p + 9]
        def SQB(p): return sv[:, 10 * p + 9: 10 * p + 10]

        def small_chain(o, p, extra_banks=False):
            if extra_banks:
                op(o, "vector", lambda e: e.tensor_add(SX(p), SX(p), SXB(p)))
                op(o, "vector", lambda e: e.tensor_add(SQ(p), SQ(p), SQB(p)))
                op(o, "vector", lambda e: e.drain())
            op(o, "vector", lambda e: e.tensor_scalar_mul(MEAN(p), SX(p), 1.0 / C))
            op(o, "vector", lambda e: e.tensor_scalar_mul(NMEAN(p), SX(p), -1.0 / C))
            op(o, "vector", lambda e: e.drain())
            op(o, "vector", lambda e: e.tensor_mul(TMPV(p), MEAN(p), MEAN(p)))
            op(o, "vector", lambda e: e.tensor_scalar_mul(VAR(p), SQ(p), 1.0 / C))
            op(o, "vector", lambda e: e.drain())
            op(o, "vector", lambda e: e.tensor_sub(VAR(p), VAR(p), TMPV(p)))
            v2 = ev(o, "vector", lambda e: e.tensor_copy(dum[:, 0:1], consts[:, 0:1]))
            wait(o, "scalar", "v", v2)
            op(o, "scalar", lambda e: e.activation(
                RSTD(p), VAR(p), AF.Sqrt, bias=float(EPS), scale=1.0))
            s2 = ev(o, "scalar", lambda e: e.copy(dum[:, 2:3], consts[:, 0:1]))
            wait(o, "vector", "s", s2)
            op(o, "vector", lambda e: e.reciprocal(RSTD(p), RSTD(p)))
            op(o, "vector", lambda e: e.drain())
            op(o, "vector", lambda e: e.tensor_mul(NBIAS(p), NMEAN(p), RSTD(p)))
            v3 = ev(o, "vector", lambda e: e.tensor_copy(dum[:, 0:1], consts[:, 0:1]))
            return v3

        o = Ops()
        op(o, "vector", lambda e: e.memset(consts[:, :], 0.0))
        op(o, "vector", lambda e: e.memset(consts[:, 1:2], 1.0))
        op(o, "vector", lambda e: e.memset(consts[:, 2:3], float(EPS)))
        op(o, "vector", lambda e: e.memset(qstage[:, :], 0.0))
        zdone = ev(o, "vector", lambda e: e.memset(lnb[:, :], 0.0))
        dma(o, idb[:, :], idb_h.ap())
        wait(o, "sync", "v", zdone)
        dma(o, ln1tm.ap()[0:4, :], lnb[0:4, 0:C])
        dma(o, ln1tm.ap()[S + 4:S + 8, :], lnb[0:4, 0:C])
        dma(o, qdr.ap()[0:4, :], qstage[0:4, 0:1296])
        pad_done = dma(o, qdr.ap()[S + 4:S + 8, :], qstage[0:4, 0:1296])
        dma(o, ln3fm.ap().rearrange("(c p) s -> p c s", p=128)[:, :, 0:1],
            lnb[:, 0:NCH][:, :, None], allow_slow_non_contiguous=True)
        cast_done = 0
        for wsrc, wdst in ((wrT_h, wr_sb), (wkT_h, wk_sb), (wvT_h, wv_sb)):
            for c in range(NCH):
                if cast_done:
                    wait(o, "sync", "g", cast_done)
                dl = dma(o, junk[:, :], wsrc.ap()[128 * c:128 * (c + 1), :])
                wait(o, "gpsimd", "d", dl)
                cast_done = dma(o, wdst[:, c * C:(c + 1) * C], junk[:, :], eng="gpsimd")
        wait(o, "sync", "d", pad_done)
        dl = dma(o, qstage[:, 0:NCH * 162].rearrange("p (c n) -> p c n", c=NCH),
                 cwq_h.ap().rearrange("c p n -> p c n"))
        wait(o, "gpsimd", "d", dl)
        dma(o, cwq_sb[:, :], qstage[:, 0:NCH * 162], eng="gpsimd")
        dma(o, mur_sb[:, :], mur_h.ap())
        dma(o, muk_sb[:, :], muk_h.ap())
        o.emit()

        def ln_phase(src_h, tm_dst, fm_dst, fm_col0):
            o = Ops()
            dx = [0] * NB
            sq = [0] * NB
            red = [0] * NB
            apl = [0] * NB
            tr = [0] * NB
            evc = [0] * NB
            tms = [0] * NB
            fms = [0] * NB
            v3a = [0] * NB

            def stage_a(i):
                p = i & 1
                if i >= 2:
                    wait(o, "sync", "s", apl[i - 2])
                dx[i] = dma(o, xbuf[:, p * C:(p + 1) * C],
                            src_h.ap()[128 * i:128 * (i + 1), :])
                wait(o, "vector", "d", dx[i])
                red[i] = ev(o, "vector", lambda e, p2=p: e.reduce_sum(
                    SX(p2), xbuf[:, p2 * C:(p2 + 1) * C], AX.X))
                wait(o, "scalar", "d", dx[i])
                wait(o, "scalar", "v", red[i])
                if i >= 2:
                    wait(o, "scalar", "v", v3a[i - 2])
                op(o, "scalar", lambda e, p2=p: e.activation(
                    junk[:, :], xbuf[:, p2 * C:(p2 + 1) * C], AF.Square,
                    accum_out=SQ(p2)))
                sq[i] = ev(o, "scalar", lambda e: e.copy(dum[:, 2:3], consts[:, 0:1]))

            def stage_b(i):
                p = i & 1
                wait(o, "vector", "s", sq[i])
                v3 = small_chain(o, p)
                v3a[i] = v3
                wait(o, "scalar", "v", v3)
                if i >= 2:
                    wait(o, "scalar", "t", tr[i - 2])
                    if tm_dst is not None:
                        wait(o, "scalar", "d", tms[i - 2])
                op(o, "scalar", lambda e, p2=p: e.activation(
                    lnb[:, p2 * C:(p2 + 1) * C], xbuf[:, p2 * C:(p2 + 1) * C],
                    AF.Identity, bias=NBIAS(p2), scale=RSTD(p2)))
                apl[i] = ev(o, "scalar", lambda e: e.copy(dum[:, 2:3], consts[:, 0:1]))
                if tm_dst is not None:
                    wait(o, "sync", "s", apl[i])
                    tms[i] = dma(o, tm_dst.ap()[4 + 128 * i:4 + 128 * (i + 1), :],
                                 lnb[:, p * C:(p + 1) * C])
                wait(o, "tensor", "s", apl[i])
                if i >= 2:
                    wait(o, "tensor", "v", evc[i - 2])
                pst = psT0 if p == 0 else psT1
                for c in range(NCH):
                    fn = lambda e, c2=c, p2=p, ps=pst: e.transpose(
                        ps[:, 128 * c2:128 * (c2 + 1)],
                        lnb[:, p2 * C + 128 * c2:p2 * C + 128 * (c2 + 1)], idb[:, :])
                    if c == NCH - 1:
                        tr[i] = ev(o, "tensor", fn)
                    else:
                        op(o, "tensor", fn)
                wait(o, "vector", "t", tr[i])
                if i >= 2:
                    wait(o, "vector", "d", fms[i - 2])
                evc[i] = ev(o, "vector", lambda e, p2=p, ps=pst: e.tensor_copy(
                    fmst[:, p2 * C:(p2 + 1) * C], ps[:, :]))
                wait(o, "sync", "v", evc[i])
                fms[i] = dma(
                    o, fm_dst.ap().rearrange("(c p) s -> p c s", p=128)
                    [:, :, fm_col0 + 128 * i:fm_col0 + 128 * (i + 1)],
                    fmst[:, p * C:(p + 1) * C].rearrange("p (c s) -> p c s", c=NCH))

            for i in range(NB):
                stage_a(i)
                if i >= 1:
                    stage_b(i - 1)
            stage_b(NB - 1)
            o.emit()

        ln_phase(xin, ln1tm, ln1fm, 0)

        o = Ops()
        mml = {}
        evq = {}
        qst = [0] * NB
        for i in range(NB):
            p = i & 1
            if i >= 2:
                wait(o, "sync", "t", mml[(i - 2, NCH - 1)])
            dw = dma(o, fmwin[:, p * NCH * 128:(p + 1) * NCH * 128]
                     .rearrange("p (c s) -> p c s", c=NCH),
                     ln1fm.ap().rearrange("(c p) s -> p c s", p=128)
                     [:, :, 128 * i:128 * (i + 1)])
            wait(o, "tensor", "d", dw)
            for c in range(NCH):
                ps = psQa if (c % 2 == 0) else psQb
                if i > 0 or c >= 2:
                    prev = evq[(i, c - 2)] if c >= 2 else evq[(i - 1, NCH - 2 + c)]
                    wait(o, "tensor", "s", prev)
                mml[(i, c)] = ev(o, "tensor", lambda e, c2=c, p2=p, ps2=ps: e.matmul(
                    ps2[:, :],
                    fmwin[:, p2 * NCH * 128 + 128 * c2:p2 * NCH * 128 + 128 * (c2 + 1)],
                    cwq_sb[:, 162 * c2:162 * (c2 + 1)], start=True, stop=True))
                wait(o, "scalar", "t", mml[(i, c)])
                if c == 0 and i >= 2:
                    wait(o, "scalar", "d", qst[i - 2])
                evq[(i, c)] = ev(o, "scalar", lambda e, c2=c, p2=p, ps2=ps: e.activation(
                    qstage[:, p2 * 1296:(p2 + 1) * 1296]
                    .rearrange("p (t h j) -> p t h j", t=9, h=H)[:, :, 2 * c2:2 * c2 + 2, :],
                    ps2[:, :].rearrange("p (t hl j) -> p t hl j", t=9, hl=2),
                    AF.Identity, bias=0.0, scale=1.0))
            wait(o, "sync", "s", evq[(i, NCH - 1)])
            qst[i] = dma(o, qdr.ap()[4 + 128 * i:4 + 128 * (i + 1), :],
                         qstage[:, p * 1296:(p + 1) * 1296])
        o.emit()

        o = Ops()
        addv = [0] * NB
        sm_end = [0] * NB
        wts = [0] * NB
        for i in range(NB):
            p = i & 1
            if i >= 2:
                wait(o, "sync", "p", addv[i - 2])
            dq = 0
            for t in range(K9):
                dq = dma(o, qstage[:, p * 1296 + 144 * t:p * 1296 + 144 * (t + 1)],
                         qdr.ap()[128 * i + t:128 * i + t + 128, 144 * t:144 * (t + 1)])
            wait(o, "gpsimd", "d", dq)
            if i >= 2:
                wait(o, "gpsimd", "v", sm_end[i - 2])
            q0 = p * 1296
            op(o, "gpsimd", lambda e, p2=p, q2=q0: e.tensor_add(
                acc[:, p2 * 144:(p2 + 1) * 144], qstage[:, q2:q2 + 144],
                qstage[:, q2 + 144:q2 + 288]))
            for t in range(2, K9):
                fn = lambda e, p2=p, q2=q0, t2=t: e.tensor_add(
                    acc[:, p2 * 144:(p2 + 1) * 144], acc[:, p2 * 144:(p2 + 1) * 144],
                    qstage[:, q2 + 144 * t2:q2 + 144 * (t2 + 1)])
                if t == K9 - 1:
                    addv[i] = ev(o, "gpsimd", fn)
                else:
                    op(o, "gpsimd", fn)
            av = acc[:, p * 144:(p + 1) * 144].rearrange("p (h j) -> p h j", h=H)
            wait(o, "vector", "p", addv[i])
            op(o, "vector", lambda e, a=av, p2=p: e.reduce_max(
                mx[:, p2 * H:(p2 + 1) * H][:, :, None], a, AX.X))
            op(o, "vector", lambda e: e.drain())
            v1 = ev(o, "vector", lambda e, a=av, p2=p: e.tensor_sub(
                a, a, mx[:, p2 * H:(p2 + 1) * H][:, :, None].broadcast_to([128, H, 9])))
            wait(o, "scalar", "v", v1)
            se = ev(o, "scalar", lambda e, p2=p: e.activation(
                acc[:, p2 * 144:(p2 + 1) * 144], acc[:, p2 * 144:(p2 + 1) * 144],
                AF.Exp, bias=0.0, scale=1.0))
            wait(o, "vector", "s", se)
            op(o, "vector", lambda e, a=av, p2=p: e.reduce_sum(
                zsum[:, p2 * H:(p2 + 1) * H][:, :, None], a, AX.X))
            op(o, "vector", lambda e: e.drain())
            op(o, "vector", lambda e, p2=p: e.reciprocal(
                zsum[:, p2 * H:(p2 + 1) * H], zsum[:, p2 * H:(p2 + 1) * H]))
            op(o, "vector", lambda e: e.drain())
            if i >= 2:
                wait(o, "vector", "d", wts[i - 2])
            sm_end[i] = ev(o, "vector", lambda e, a=av, p2=p: e.tensor_mul(
                wtb[:, p2 * 144:(p2 + 1) * 144].rearrange("p (h j) -> p h j", h=H),
                a, zsum[:, p2 * H:(p2 + 1) * H][:, :, None].broadcast_to([128, H, 9])))
            wait(o, "sync", "v", sm_end[i])
            wts[i] = dma(o, wtd.ap()[128 * i:128 * (i + 1), :],
                         wtb[:, p * 144:(p + 1) * 144])
        o.emit()

        o = Ops()
        x2_store = 0
        for i in range(NB):
            if i > 0:
                wait(o, "sync", "v", mul_done)
                wait(o, "sync", "v", res_done)
            dw = dma(o, wtb[:, 0:144], wtd.ap()[128 * i:128 * (i + 1), :])
            dx = dma(o, xbuf[:, 0:C], xin.ap()[128 * i:128 * (i + 1), :])
            dk = 0
            for k in range(K9):
                dk = dma(o, xk9[:, C * k:C * (k + 1)],
                         ln1tm.ap()[128 * i + k:128 * i + k + 128, :])
            wait(o, "vector", "d", dk)
            wait(o, "vector", "d", dw)
            for k in range(K9):
                src = xk9[:, C * k:C * (k + 1)].rearrange("p (h d) -> p h d", h=H)
                wslice = wtb[:, 0:144].rearrange("p (h j) -> p h j", h=H)[:, :, k:k + 1]
                if k == 0:
                    op(o, "vector", lambda e, a=src, b=wslice: e.tensor_mul(
                        x2b[:, 0:C].rearrange("p (h d) -> p h d", h=H), a,
                        b.broadcast_to([128, H, HD])))
                else:
                    op(o, "vector", lambda e, a=src, b=wslice: e.tensor_mul(
                        tmp[:, 0:C].rearrange("p (h d) -> p h d", h=H), a,
                        b.broadcast_to([128, H, HD])))
                    fn = lambda e: e.tensor_add(x2b[:, 0:C], x2b[:, 0:C], tmp[:, 0:C])
                    if k == K9 - 1:
                        mul_done = ev(o, "vector", fn)
                    else:
                        op(o, "vector", fn)
            wait(o, "scalar", "v", mul_done)
            v1 = ev(o, "vector", lambda e: e.reduce_sum(SX(0), x2b[:, 0:C], AX.X))
            wait(o, "scalar", "v", v1)
            s1 = ev(o, "scalar", lambda e: e.activation(
                junk[:, :], x2b[:, 0:C], AF.Square, accum_out=SQ(0)))
            wait(o, "vector", "s", s1)
            op(o, "vector", lambda e: e.tensor_scalar_mul(MEAN(0), SX(0), 1.0 / C))
            op(o, "vector", lambda e: e.tensor_scalar_mul(NMEAN(0), SX(0), -1.0 / C))
            op(o, "vector", lambda e: e.drain())
            op(o, "vector", lambda e: e.tensor_mul(TMPV(0), MEAN(0), MEAN(0)))
            op(o, "vector", lambda e: e.tensor_scalar_mul(VAR(0), SQ(0), 1.0 / C))
            op(o, "vector", lambda e: e.drain())
            v2 = ev(o, "vector", lambda e: e.tensor_sub(VAR(0), VAR(0), TMPV(0)))
            wait(o, "scalar", "v", v2)
            s2 = ev(o, "scalar", lambda e: e.activation(
                RSTD(0), VAR(0), AF.Sqrt, bias=float(EPS), scale=1.0))
            wait(o, "vector", "s", s2)
            op(o, "vector", lambda e: e.reciprocal(RSTD(0), RSTD(0)))
            op(o, "vector", lambda e: e.drain())
            v3 = ev(o, "vector", lambda e: e.tensor_mul(NBIAS(0), NMEAN(0), RSTD(0)))
            wait(o, "scalar", "v", v3)
            s3 = ev(o, "scalar", lambda e: e.activation(
                tmp[:, 0:C], x2b[:, 0:C], AF.Identity, bias=NBIAS(0), scale=RSTD(0)))
            wait(o, "vector", "s", s3)
            res_done = ev(o, "vector", lambda e: e.tensor_add(x2b[:, C:2 * C], tmp[:, 0:C], xbuf[:, 0:C]))
            wait(o, "sync", "v", res_done)
            x2_store = dma(o, x2d.ap()[128 * i:128 * (i + 1), :], x2b[:, C:2 * C])
        o.emit()

        ln_phase(x2d, None, ln3fm, 1)

        o = Ops()
        NIT = NCH * 4
        xr_ev = [0] * NIT
        xk_ev = [0] * NIT
        st_xr = [0] * NIT
        st_xk = [0] * NIT
        it = 0
        for c in range(NCH):
            for hh in range(4):
                p = it & 1
                if it >= 2:
                    wait(o, "sync", "v", xk_ev[it - 2])
                dc = dma(o, fmc[:, p * (C + 1):(p + 1) * (C + 1)],
                         ln3fm.ap().rearrange("(c p) s -> p c s", p=128)
                         [:, c, C * hh:C * hh + C + 1])
                wait(o, "vector", "d", dc)
                dv = ev(o, "vector", lambda e, p2=p: e.tensor_sub(
                    diff[:, :], fmc[:, p2 * (C + 1) + 1:(p2 + 1) * (C + 1)],
                    fmc[:, p2 * (C + 1):p2 * (C + 1) + C]))
                if it >= 2:
                    wait(o, "vector", "d", st_xr[it - 2])
                xr_ev[it] = ev(o, "vector", lambda e, c2=c, p2=p: e.scalar_tensor_tensor(
                    xrb[:, p2 * C:(p2 + 1) * C], diff[:, :], mur_sb[:, c2:c2 + 1],
                    fmc[:, p2 * (C + 1):p2 * (C + 1) + C], ALU.mult, ALU.add))
                if it >= 2:
                    wait(o, "vector", "d", st_xk[it - 2])
                xk_ev[it] = ev(o, "vector", lambda e, c2=c, p2=p: e.scalar_tensor_tensor(
                    xkb[:, p2 * C:(p2 + 1) * C], diff[:, :], muk_sb[:, c2:c2 + 1],
                    fmc[:, p2 * (C + 1):p2 * (C + 1) + C], ALU.mult, ALU.add))
                wait(o, "sync", "v", xr_ev[it])
                st_xr[it] = dma(o, xrd.ap()[128 * c:128 * (c + 1), C * hh:C * (hh + 1)],
                                xrb[:, p * C:(p + 1) * C])
                wait(o, "sync", "v", xk_ev[it])
                st_xk[it] = dma(o, xkd.ap()[128 * c:128 * (c + 1), C * hh:C * (hh + 1)],
                                xkb[:, p * C:(p + 1) * C])
                it += 1
        o.emit()

        o = Ops()
        banks = (psA, psB, psC, psD)
        res_prev = [0] * 32
        out_prev = [0] * 32
        gt_last_v = 0
        v_mm_last = 0
        ln4_sq_prev = 0
        for ti in range(8):
            s0 = 512 * ti
            if ti > 0:
                wait(o, "sync", "t", v_mm_last)
                wait(o, "sync", "v", gt_last_v)
            dxr = dma(o, xr_sb[:, :].rearrange("p (c s) -> p c s", c=NCH),
                      xrd.ap().rearrange("(c p) s -> p c s", p=128)[:, :, s0:s0 + 512])
            dxk = dma(o, xk_sb[:, :].rearrange("p (c s) -> p c s", c=NCH),
                      xkd.ap().rearrange("(c p) s -> p c s", p=128)[:, :, s0:s0 + 512])
            wait(o, "tensor", "d", dxk)
            kk_evs = [0] * NCH
            kact_evs = [0] * NCH
            for d in range(NCH):
                ps = banks[d % 4]
                if ti > 0 and d < 4:
                    wait(o, "tensor", "v", gt_last_v)
                if d >= 4:
                    wait(o, "tensor", "v", kk_evs[d - 4])
                for c in range(NCH):
                    fn = lambda e, c2=c, d2=d, p2=ps: e.matmul(
                        p2[:, :], wk_sb[:, c2 * C + 128 * d2: c2 * C + 128 * (d2 + 1)],
                        xk_sb[:, 512 * c2:512 * (c2 + 1)],
                        start=(c2 == 0), stop=(c2 == NCH - 1))
                    if c == NCH - 1:
                        mmk = ev(o, "tensor", fn)
                    else:
                        op(o, "tensor", fn)
                wait(o, "vector", "t", mmk)
                pk = d & 1
                if d >= 2:
                    wait(o, "vector", "s", kact_evs[d - 2])
                kk_evs[d] = ev(o, "vector", lambda e, p2=ps, pk2=pk: e.tensor_scalar_max(
                    kkb[:, pk2 * 512:(pk2 + 1) * 512], p2[:, :], 0.0))
                wait(o, "scalar", "v", kk_evs[d])
                kact_evs[d] = ev(o, "scalar", lambda e, d2=d, pk2=pk: e.activation(
                    kact[:, 512 * d2:512 * (d2 + 1)], kkb[:, pk2 * 512:(pk2 + 1) * 512],
                    AF.Square))
            wait(o, "tensor", "d", dxr)
            r_evs = [0] * 8
            for u in range(4):
                for hf in range(2):
                    idx = 2 * u + hf
                    ps = banks[idx % 4]
                    if idx < 4:
                        wait(o, "tensor", "v", kk_evs[4 + idx])
                    else:
                        wait(o, "tensor", "s", r_evs[idx - 4])
                    for c in range(NCH):
                        fn = lambda e, c2=c, u2=u, h2=hf, p2=ps: e.matmul(
                            p2[:, :], xr_sb[:, 512 * c2 + 128 * u2: 512 * c2 + 128 * (u2 + 1)],
                            wr_sb[:, c2 * C + 512 * h2: c2 * C + 512 * (h2 + 1)],
                            start=(c2 == 0), stop=(c2 == NCH - 1))
                        if c == NCH - 1:
                            mmr = ev(o, "tensor", fn)
                        else:
                            op(o, "tensor", fn)
                    wait(o, "scalar", "t", mmr)
                    r_evs[idx] = ev(o, "scalar", lambda e, u2=u, h2=hf, p2=ps: e.activation(
                        r_sb[:, C * u2 + 512 * h2: C * u2 + 512 * (h2 + 1)],
                        p2[:, :], AF.Sigmoid, bias=0.0, scale=1.0))
            v_evs = [0] * 8
            for u in range(4):
                pu = 0
                g = ti * 4 + u
                if g >= 1:
                    wait(o, "sync", "v", res_prev[g - 1])
                dx2u = dma(o, x2t[:, :],
                           x2d.ap()[s0 + 128 * u:s0 + 128 * (u + 1), :])
                for hf in range(2):
                    idx = 2 * u + hf
                    ps = banks[idx % 4]
                    if idx == 0:
                        wait(o, "tensor", "s", kact_evs[NCH - 1])
                    if idx < 4:
                        wait(o, "tensor", "s", r_evs[4 + idx])
                    else:
                        wait(o, "tensor", "v", v_evs[idx - 4])
                    for d in range(NCH):
                        fn = lambda e, d2=d, u2=u, h2=hf, p2=ps: e.matmul(
                            p2[:, :], kact[:, 512 * d2 + 128 * u2: 512 * d2 + 128 * (u2 + 1)],
                            wv_sb[:, d2 * C + 512 * h2: d2 * C + 512 * (h2 + 1)],
                            start=(d2 == 0), stop=(d2 == NCH - 1))
                        if d == NCH - 1:
                            mmv = ev(o, "tensor", fn)
                        else:
                            op(o, "tensor", fn)
                    wait(o, "vector", "t", mmv)
                    if g > 0 or hf > 0:
                        wait(o, "vector", "s", ln4_sq_prev)
                    v_evs[idx] = ev(o, "vector", lambda e, u2=u, h2=hf, p2=ps: e.tensor_mul(
                        ffn[:, C * u2 + 512 * h2: C * u2 + 512 * (h2 + 1)],
                        p2[:, :], r_sb[:, C * u2 + 512 * h2: C * u2 + 512 * (h2 + 1)]))
                    v_mm_last = mmv
                gt_last_v = v_evs[2 * u + 1]
                pp = g & 1
                fsl = ffn[:, C * u:C * (u + 1)]
                v1 = ev(o, "vector", lambda e, a=fsl, p2=pp: e.reduce_sum(SX(p2), a, AX.X))
                wait(o, "scalar", "v", v1)
                ln4_sq_prev = ev(o, "scalar", lambda e, a=fsl, p2=pp: e.activation(
                    junk[:, :], a, AF.Square, accum_out=SQ(p2)))
                wait(o, "vector", "s", ln4_sq_prev)
                v3 = small_chain(o, pp)
                wait(o, "scalar", "v", v3)
                if g >= 2:
                    wait(o, "scalar", "v", res_prev[g - 2])
                s3 = ev(o, "scalar", lambda e, a=fsl, p2=pp: e.activation(
                    tmp[:, p2 * C:(p2 + 1) * C], a, AF.Identity,
                    bias=NBIAS(p2), scale=RSTD(p2)))
                wait(o, "vector", "s", s3)
                wait(o, "vector", "d", dx2u)
                if g >= 1:
                    wait(o, "vector", "d", out_prev[g - 1])
                res_prev[g] = ev(o, "vector", lambda e, pu2=pu, p2=pp: e.tensor_add(
                    outb[:, :], tmp[:, p2 * C:(p2 + 1) * C],
                    x2t[:, pu2 * C:(pu2 + 1) * C]))
                wait(o, "sync", "v", res_prev[g])
                out_prev[g] = dma(
                    o, out_h.ap()[s0 + 128 * u:s0 + 128 * (u + 1), :],
                    outb[:, :])
        wait(o, "sync", "d", ct["d"])
        wait(o, "sync", "g", ct["g"])
        o.emit()

    return nc


_NC_CACHE = {}


def _get_nc(debug=False):
    if debug not in _NC_CACHE:
        _NC_CACHE[debug] = build_nc(debug)
    return _NC_CACHE[debug]


def kernel(L0_attn_query, L0_attn_conv_weight, L0_ffn_W_r, L0_ffn_W_k,
           L0_ffn_W_v, L0_ffn_mu_r, L0_ffn_mu_k):
    x = np.ascontiguousarray(np.asarray(L0_attn_query, dtype=np.float32))
    cw = np.asarray(L0_attn_conv_weight, dtype=np.float32)
    wr = np.asarray(L0_ffn_W_r, dtype=np.float32)
    wk = np.asarray(L0_ffn_W_k, dtype=np.float32)
    wv = np.asarray(L0_ffn_W_v, dtype=np.float32)
    mur = np.asarray(L0_ffn_mu_r, dtype=np.float32)
    muk = np.asarray(L0_ffn_mu_k, dtype=np.float32)

    wrT = np.ascontiguousarray(wr.T)
    wkT = np.ascontiguousarray(wk.T)
    wvT = np.ascontiguousarray(wv.T)
    cwq = np.zeros((NCH, 128, 162), dtype=np.float32)
    cwr = cw.reshape(H, 9, HD, K9)
    for cc in range(NCH):
        for hl in range(2):
            h = 2 * cc + hl
            blk = np.transpose(cwr[h], (1, 2, 0))
            cols = (np.arange(K9)[:, None] * 18 + hl * 9 + np.arange(9)[None, :])
            cwq[cc, 64 * hl:64 * (hl + 1), cols.reshape(-1)] = blk.reshape(HD, 81).T
    mur8 = np.ascontiguousarray(mur.reshape(NCH, 128).T)
    muk8 = np.ascontiguousarray(muk.reshape(NCH, 128).T)
    idb = np.eye(128, dtype=np.float32).astype(ml_dtypes.bfloat16)

    nc = _get_nc()
    in_maps = []
    for b in range(NCORES):
        in_maps.append({
            "xin": np.ascontiguousarray(x[b]),
            "wrT": wrT, "wkT": wkT, "wvT": wvT,
            "cwq": cwq, "mur": mur8, "muk": muk8, "idbh": idb,
        })
    res = run_bass_kernel_spmd(nc, in_maps, core_ids=list(range(NCORES)))
    out = np.stack([np.asarray(r["out"]) for r in res.results], axis=0)
    return out.astype(np.float32)


# version 21
# speedup vs baseline: 1.0562x; 1.0562x over previous
import contextlib
import numpy as np
import ml_dtypes
import concourse.bass as bass
import concourse.mybir as mybir
from concourse.bass_utils import run_bass_kernel_spmd

F32 = mybir.dt.float32
BF16 = mybir.dt.bfloat16
AF = mybir.ActivationFunctionType
ALU = mybir.AluOpType
AX = mybir.AxisListType

S = 4096
C = 1024
NB = S // 128
NCH = 8
H = 16
HD = 64
K9 = 9
EPS = 1e-5
NCORES = 8


def build_nc(debug=False):
    nc = bass.Bass("TRN2", target_bir_lowering=False)

    xin = nc.dram_tensor("xin", [S, C], F32, kind="ExternalInput")
    wrT_h = nc.dram_tensor("wrT", [C, C], F32, kind="ExternalInput")
    wkT_h = nc.dram_tensor("wkT", [C, C], F32, kind="ExternalInput")
    wvT_h = nc.dram_tensor("wvT", [C, C], F32, kind="ExternalInput")
    cwq_h = nc.dram_tensor("cwq", [NCH, 128, 162], F32, kind="ExternalInput")
    mur_h = nc.dram_tensor("mur", [128, NCH], F32, kind="ExternalInput")
    muk_h = nc.dram_tensor("muk", [128, NCH], F32, kind="ExternalInput")
    idb_h = nc.dram_tensor("idbh", [128, 128], BF16, kind="ExternalInput")
    out_h = nc.dram_tensor("out", [S, C], F32, kind="ExternalOutput")

    kd = "ExternalOutput" if debug else "Internal"
    ln1tm = nc.dram_tensor("ln1tm", [S + 8, C], BF16, kind=kd)
    ln1fm = nc.dram_tensor("ln1fm", [C, S], BF16, kind=kd)
    qdr = nc.dram_tensor("qdr", [S + 8, 9 * 144], F32, kind=kd)
    wtd = nc.dram_tensor("wtd", [S, 144], BF16, kind=kd)
    x2d = nc.dram_tensor("x2d", [S, C], F32, kind=kd)
    ln3fm = nc.dram_tensor("ln3fm", [C, S + 1], BF16, kind=kd)
    xrd = nc.dram_tensor("xrd", [C, S], BF16, kind=kd)
    xkd = nc.dram_tensor("xkd", [C, S], BF16, kind=kd)
    tmpdbg = nc.dram_tensor("tmpdbg", [S, C], F32, kind=kd)
    ydbg = nc.dram_tensor("ydbg", [S, C], F32, kind=kd)
    svdbg = nc.dram_tensor("svdbg", [S, 20], F32, kind=kd)

    with contextlib.ExitStack() as st:
        def sb(name, shape, dt=F32):
            return st.enter_context(nc.sbuf_tensor(name, shape, dt))

        wr_sb = sb("wr_sb", [128, NCH * C], BF16)
        wk_sb = sb("wk_sb", [128, NCH * C], BF16)
        wv_sb = sb("wv_sb", [128, NCH * C], BF16)
        cwq_sb = sb("cwq_sb", [128, NCH * 162], BF16)
        mur_sb = sb("mur_sb", [128, NCH])
        muk_sb = sb("muk_sb", [128, NCH])
        idb = sb("idb", [128, 128], BF16)
        consts = sb("consts", [128, 4])

        xbuf = sb("xbuf", [128, 2 * C])
        lnb = sb("lnb", [128, 2 * C], BF16)
        fmst = sb("fmst", [128, 2 * C], BF16)
        junk = sb("junk", [128, C])
        sv = sb("sv", [128, 20])
        fmwin = sb("fmwin", [128, 2 * NCH * 128], BF16)
        qstage = sb("qstage", [128, 2 * 1296])
        acc = sb("acc", [128, 2 * 144])
        mx = sb("mx", [128, 2 * H])
        zsum = sb("zsum", [128, 2 * H])
        wtb = sb("wtb", [128, 2 * 144], BF16)
        xk9 = sb("xk9", [128, 9 * C], BF16)
        acc2 = sb("acc2", [128, C])
        ptmp = sb("ptmp", [128, C])
        acc2d = sb("acc2d", [128, C])
        tmp = sb("tmp", [128, 2 * C])
        x2b = sb("x2b", [128, 2 * C])
        fmc = sb("fmc", [128, 2 * (C + 1)], BF16)
        diff = sb("diff", [128, C], BF16)
        xrb = sb("xrb", [128, 2 * C], BF16)
        xkb = sb("xkb", [128, 2 * C], BF16)
        xr_sb = sb("xr_sb", [128, NCH * 512], BF16)
        xk_sb = sb("xk_sb", [128, NCH * 512], BF16)
        kact = sb("kact", [128, NCH * 512], BF16)
        kkb = sb("kkb", [128, 2 * 512])
        r_sb = sb("r_sb", [128, 4 * C], BF16)
        ffn = sb("ffn", [128, 4 * C])
        x2t = sb("x2t", [128, C])
        outb = sb("outb", [128, C])
        dum = sb("dum", [128, 4])

        psQa = st.enter_context(nc.psum_tensor("psQa", [128, 162], F32))
        psQb = st.enter_context(nc.psum_tensor("psQb", [128, 162], F32))
        psT0 = st.enter_context(nc.psum_tensor("psT0", [128, C], BF16))
        psT1 = st.enter_context(nc.psum_tensor("psT1", [128, C], BF16))
        psA = st.enter_context(nc.psum_tensor("psA", [128, 512], F32))
        psB = st.enter_context(nc.psum_tensor("psB", [128, 512], F32))
        psC = st.enter_context(nc.psum_tensor("psC", [128, 512], F32))
        psD = st.enter_context(nc.psum_tensor("psD", [128, 512], F32))

        dsm = st.enter_context(nc.semaphore("dsm"))
        gsm = st.enter_context(nc.semaphore("gsm"))
        vsm = st.enter_context(nc.semaphore("vsm"))
        ssm = st.enter_context(nc.semaphore("ssm"))
        tsm = st.enter_context(nc.semaphore("tsm"))
        psm = st.enter_context(nc.semaphore("psm"))

        nc.const_aps.register(F32, 0.0, consts[:, 0:1])
        nc.const_aps.register(F32, 1.0, consts[:, 1:2])
        nc.const_aps.register(F32, float(EPS), consts[:, 2:3])

        ct = dict(d=0, g=0, v=0, s=0, t=0, p=0)
        SEMS = dict(d=dsm, g=gsm, v=vsm, s=ssm, t=tsm, p=psm)
        EKEY = dict(vector="v", scalar="s", tensor="t", gpsimd="p")

        class Ops:
            def __init__(self):
                self.l = dict(sync=[], vector=[], scalar=[], tensor=[], gpsimd=[])

            def add(self, eng, fn):
                self.l[eng].append(fn)

            def emit(self):
                with nc.Block() as blk:
                    for eng, dec in (("sync", blk.sync), ("vector", blk.vector),
                                     ("scalar", blk.scalar), ("tensor", blk.tensor),
                                     ("gpsimd", blk.gpsimd)):
                        if self.l[eng]:
                            dec(lambda e, L=self.l[eng]: [f(e) for f in L])

        def dma(ops, dst, src, eng="sync", **kw):
            key = "g" if eng == "gpsimd" else "d"
            sem = SEMS[key]
            ct[key] += 16
            ops.add(eng, lambda e, a=dst, b=src, s2=sem, k2=kw:
                    e.dma_start(a, b, **k2).then_inc(s2, 16))
            return ct[key]

        def wait(ops, eng, sem_key, val):
            if val:
                ops.add(eng, lambda e, s2=SEMS[sem_key], v2=val: e.wait_ge(s2, v2))

        def ev(ops, eng, fn):
            key = EKEY[eng]
            sem = SEMS[key]
            ct[key] += 1
            ops.add(eng, lambda e, f=fn, s2=sem: f(e).then_inc(s2, 1))
            return ct[key]

        def op(ops, eng, fn):
            ops.add(eng, fn)

        def SX(p): return sv[:, 10 * p + 0: 10 * p + 1]
        def SQ(p): return sv[:, 10 * p + 1: 10 * p + 2]
        def MEAN(p): return sv[:, 10 * p + 2: 10 * p + 3]
        def NMEAN(p): return sv[:, 10 * p + 3: 10 * p + 4]
        def VAR(p): return sv[:, 10 * p + 4: 10 * p + 5]
        def RSTD(p): return sv[:, 10 * p + 5: 10 * p + 6]
        def NBIAS(p): return sv[:, 10 * p + 6: 10 * p + 7]
        def TMPV(p): return sv[:, 10 * p + 7: 10 * p + 8]
        def SXB(p): return sv[:, 10 * p + 8: 10 * p + 9]
        def SQB(p): return sv[:, 10 * p + 9: 10 * p + 10]

        def small_chain(o, p, extra_banks=False):
            if extra_banks:
                op(o, "vector", lambda e: e.tensor_add(SX(p), SX(p), SXB(p)))
                op(o, "vector", lambda e: e.tensor_add(SQ(p), SQ(p), SQB(p)))
                op(o, "vector", lambda e: e.drain())
            op(o, "vector", lambda e: e.tensor_scalar_mul(MEAN(p), SX(p), 1.0 / C))
            op(o, "vector", lambda e: e.tensor_scalar_mul(NMEAN(p), SX(p), -1.0 / C))
            op(o, "vector", lambda e: e.drain())
            op(o, "vector", lambda e: e.tensor_mul(TMPV(p), MEAN(p), MEAN(p)))
            op(o, "vector", lambda e: e.tensor_scalar_mul(VAR(p), SQ(p), 1.0 / C))
            op(o, "vector", lambda e: e.drain())
            op(o, "vector", lambda e: e.tensor_sub(VAR(p), VAR(p), TMPV(p)))
            v2 = ev(o, "vector", lambda e: e.tensor_copy(dum[:, 0:1], consts[:, 0:1]))
            wait(o, "scalar", "v", v2)
            op(o, "scalar", lambda e: e.activation(
                RSTD(p), VAR(p), AF.Sqrt, bias=float(EPS), scale=1.0))
            s2 = ev(o, "scalar", lambda e: e.copy(dum[:, 2:3], consts[:, 0:1]))
            wait(o, "vector", "s", s2)
            op(o, "vector", lambda e: e.reciprocal(RSTD(p), RSTD(p)))
            op(o, "vector", lambda e: e.drain())
            op(o, "vector", lambda e: e.tensor_mul(NBIAS(p), NMEAN(p), RSTD(p)))
            v3 = ev(o, "vector", lambda e: e.tensor_copy(dum[:, 0:1], consts[:, 0:1]))
            return v3

        o = Ops()
        op(o, "vector", lambda e: e.memset(consts[:, :], 0.0))
        op(o, "vector", lambda e: e.memset(consts[:, 1:2], 1.0))
        op(o, "vector", lambda e: e.memset(consts[:, 2:3], float(EPS)))
        op(o, "vector", lambda e: e.memset(qstage[:, :], 0.0))
        zdone = ev(o, "vector", lambda e: e.memset(lnb[:, :], 0.0))
        dma(o, idb[:, :], idb_h.ap())
        wait(o, "sync", "v", zdone)
        dma(o, ln1tm.ap()[0:4, :], lnb[0:4, 0:C])
        dma(o, ln1tm.ap()[S + 4:S + 8, :], lnb[0:4, 0:C])
        dma(o, qdr.ap()[0:4, :], qstage[0:4, 0:1296])
        pad_done = dma(o, qdr.ap()[S + 4:S + 8, :], qstage[0:4, 0:1296])
        dma(o, ln3fm.ap().rearrange("(c p) s -> p c s", p=128)[:, :, 0:1],
            lnb[:, 0:NCH][:, :, None], allow_slow_non_contiguous=True)
        cast_done = 0
        for wsrc, wdst in ((wrT_h, wr_sb), (wkT_h, wk_sb), (wvT_h, wv_sb)):
            for c in range(NCH):
                if cast_done:
                    wait(o, "sync", "g", cast_done)
                dl = dma(o, junk[:, :], wsrc.ap()[128 * c:128 * (c + 1), :])
                wait(o, "gpsimd", "d", dl)
                cast_done = dma(o, wdst[:, c * C:(c + 1) * C], junk[:, :], eng="gpsimd")
        wait(o, "sync", "d", pad_done)
        dl = dma(o, qstage[:, 0:NCH * 162].rearrange("p (c n) -> p c n", c=NCH),
                 cwq_h.ap().rearrange("c p n -> p c n"))
        wait(o, "gpsimd", "d", dl)
        dma(o, cwq_sb[:, :], qstage[:, 0:NCH * 162], eng="gpsimd")
        dma(o, mur_sb[:, :], mur_h.ap())
        dma(o, muk_sb[:, :], muk_h.ap())
        o.emit()

        def ln_phase(src_h, tm_dst, fm_dst, fm_col0):
            o = Ops()
            dx = [0] * NB
            sq = [0] * NB
            red = [0] * NB
            apl = [0] * NB
            tr = [0] * NB
            evc = [0] * NB
            tms = [0] * NB
            fms = [0] * NB
            v3a = [0] * NB

            def stage_a(i):
                p = i & 1
                if i >= 2:
                    wait(o, "sync", "s", apl[i - 2])
                dx[i] = dma(o, xbuf[:, p * C:(p + 1) * C],
                            src_h.ap()[128 * i:128 * (i + 1), :])
                wait(o, "vector", "d", dx[i])
                red[i] = ev(o, "vector", lambda e, p2=p: e.reduce_sum(
                    SX(p2), xbuf[:, p2 * C:(p2 + 1) * C], AX.X))
                wait(o, "scalar", "d", dx[i])
                wait(o, "scalar", "v", red[i])
                if i >= 2:
                    wait(o, "scalar", "v", v3a[i - 2])
                op(o, "scalar", lambda e, p2=p: e.activation(
                    junk[:, :], xbuf[:, p2 * C:(p2 + 1) * C], AF.Square,
                    accum_out=SQ(p2)))
                sq[i] = ev(o, "scalar", lambda e: e.copy(dum[:, 2:3], consts[:, 0:1]))

            def stage_b(i):
                p = i & 1
                wait(o, "vector", "s", sq[i])
                v3 = small_chain(o, p)
                v3a[i] = v3
                wait(o, "scalar", "v", v3)
                if i >= 2:
                    wait(o, "scalar", "t", tr[i - 2])
                    if tm_dst is not None:
                        wait(o, "scalar", "d", tms[i - 2])
                op(o, "scalar", lambda e, p2=p: e.activation(
                    lnb[:, p2 * C:(p2 + 1) * C], xbuf[:, p2 * C:(p2 + 1) * C],
                    AF.Identity, bias=NBIAS(p2), scale=RSTD(p2)))
                apl[i] = ev(o, "scalar", lambda e: e.copy(dum[:, 2:3], consts[:, 0:1]))
                if tm_dst is not None:
                    wait(o, "sync", "s", apl[i])
                    tms[i] = dma(o, tm_dst.ap()[4 + 128 * i:4 + 128 * (i + 1), :],
                                 lnb[:, p * C:(p + 1) * C])
                wait(o, "tensor", "s", apl[i])
                if i >= 2:
                    wait(o, "tensor", "v", evc[i - 2])
                pst = psT0 if p == 0 else psT1
                for c in range(NCH):
                    fn = lambda e, c2=c, p2=p, ps=pst: e.transpose(
                        ps[:, 128 * c2:128 * (c2 + 1)],
                        lnb[:, p2 * C + 128 * c2:p2 * C + 128 * (c2 + 1)], idb[:, :])
                    if c == NCH - 1:
                        tr[i] = ev(o, "tensor", fn)
                    else:
                        op(o, "tensor", fn)
                wait(o, "vector", "t", tr[i])
                if i >= 2:
                    wait(o, "vector", "d", fms[i - 2])
                evc[i] = ev(o, "vector", lambda e, p2=p, ps=pst: e.tensor_copy(
                    fmst[:, p2 * C:(p2 + 1) * C], ps[:, :]))
                wait(o, "sync", "v", evc[i])
                fms[i] = dma(
                    o, fm_dst.ap().rearrange("(c p) s -> p c s", p=128)
                    [:, :, fm_col0 + 128 * i:fm_col0 + 128 * (i + 1)],
                    fmst[:, p * C:(p + 1) * C].rearrange("p (c s) -> p c s", c=NCH))

            for i in range(NB):
                stage_a(i)
                if i >= 1:
                    stage_b(i - 1)
            stage_b(NB - 1)
            o.emit()

        ln_phase(xin, ln1tm, ln1fm, 0)

        o = Ops()
        mml = {}
        evq = {}
        qst = [0] * NB
        for i in range(NB):
            p = i & 1
            if i >= 2:
                wait(o, "sync", "t", mml[(i - 2, NCH - 1)])
            dw = dma(o, fmwin[:, p * NCH * 128:(p + 1) * NCH * 128]
                     .rearrange("p (c s) -> p c s", c=NCH),
                     ln1fm.ap().rearrange("(c p) s -> p c s", p=128)
                     [:, :, 128 * i:128 * (i + 1)])
            wait(o, "tensor", "d", dw)
            for c in range(NCH):
                ps = psQa if (c % 2 == 0) else psQb
                if i > 0 or c >= 2:
                    prev = evq[(i, c - 2)] if c >= 2 else evq[(i - 1, NCH - 2 + c)]
                    wait(o, "tensor", "s", prev)
                mml[(i, c)] = ev(o, "tensor", lambda e, c2=c, p2=p, ps2=ps: e.matmul(
                    ps2[:, :],
                    fmwin[:, p2 * NCH * 128 + 128 * c2:p2 * NCH * 128 + 128 * (c2 + 1)],
                    cwq_sb[:, 162 * c2:162 * (c2 + 1)], start=True, stop=True))
                wait(o, "scalar", "t", mml[(i, c)])
                if c == 0 and i >= 2:
                    wait(o, "scalar", "d", qst[i - 2])
                evq[(i, c)] = ev(o, "scalar", lambda e, c2=c, p2=p, ps2=ps: e.activation(
                    qstage[:, p2 * 1296:(p2 + 1) * 1296]
                    .rearrange("p (t h j) -> p t h j", t=9, h=H)[:, :, 2 * c2:2 * c2 + 2, :],
                    ps2[:, :].rearrange("p (t hl j) -> p t hl j", t=9, hl=2),
                    AF.Identity, bias=0.0, scale=1.0))
            wait(o, "sync", "s", evq[(i, NCH - 1)])
            qst[i] = dma(o, qdr.ap()[4 + 128 * i:4 + 128 * (i + 1), :],
                         qstage[:, p * 1296:(p + 1) * 1296])
        o.emit()

        o = Ops()
        addv = [0] * NB
        sm_end = [0] * NB
        wts = [0] * NB

        def p4_a(i):
            p = i & 1
            if i >= 2:
                wait(o, "sync", "p", addv[i - 2])
            dq = 0
            for t in range(K9):
                dq = dma(o, qstage[:, p * 1296 + 144 * t:p * 1296 + 144 * (t + 1)],
                         qdr.ap()[128 * i + t:128 * i + t + 128, 144 * t:144 * (t + 1)])
            wait(o, "gpsimd", "d", dq)
            if i >= 2:
                wait(o, "gpsimd", "v", sm_end[i - 2])
            q0 = p * 1296
            op(o, "gpsimd", lambda e, p2=p, q2=q0: e.tensor_add(
                acc[:, p2 * 144:(p2 + 1) * 144], qstage[:, q2:q2 + 144],
                qstage[:, q2 + 144:q2 + 288]))
            for t in range(2, K9):
                op(o, "gpsimd", lambda e, p2=p, q2=q0, t2=t: e.tensor_add(
                    acc[:, p2 * 144:(p2 + 1) * 144], acc[:, p2 * 144:(p2 + 1) * 144],
                    qstage[:, q2 + 144 * t2:q2 + 144 * (t2 + 1)]))
            addv[i] = ev(o, "gpsimd", lambda e: e.tensor_copy(dum[:, 1:2], consts[:, 0:1]))

        def p4_b(i):
            p = i & 1
            av = acc[:, p * 144:(p + 1) * 144].rearrange("p (h j) -> p h j", h=H)
            wait(o, "vector", "p", addv[i])
            op(o, "vector", lambda e, a=av, p2=p: e.reduce_max(
                mx[:, p2 * H:(p2 + 1) * H][:, :, None], a, AX.X))
            op(o, "vector", lambda e: e.drain())
            v1 = ev(o, "vector", lambda e, a=av, p2=p: e.tensor_sub(
                a, a, mx[:, p2 * H:(p2 + 1) * H][:, :, None].broadcast_to([128, H, 9])))
            wait(o, "scalar", "v", v1)
            op(o, "scalar", lambda e, p2=p: e.activation(
                acc[:, p2 * 144:(p2 + 1) * 144], acc[:, p2 * 144:(p2 + 1) * 144],
                AF.Exp, bias=0.0, scale=1.0))
            se = ev(o, "scalar", lambda e: e.copy(dum[:, 2:3], consts[:, 0:1]))
            wait(o, "vector", "s", se)
            op(o, "vector", lambda e, a=av, p2=p: e.reduce_sum(
                zsum[:, p2 * H:(p2 + 1) * H][:, :, None], a, AX.X))
            op(o, "vector", lambda e: e.drain())
            op(o, "vector", lambda e, p2=p: e.reciprocal(
                zsum[:, p2 * H:(p2 + 1) * H], zsum[:, p2 * H:(p2 + 1) * H]))
            op(o, "vector", lambda e: e.drain())
            if i >= 2:
                wait(o, "vector", "d", wts[i - 2])
            op(o, "vector", lambda e, a=av, p2=p: e.tensor_mul(
                wtb[:, p2 * 144:(p2 + 1) * 144].rearrange("p (h j) -> p h j", h=H),
                a, zsum[:, p2 * H:(p2 + 1) * H][:, :, None].broadcast_to([128, H, 9])))
            sm_end[i] = ev(o, "vector", lambda e: e.tensor_copy(dum[:, 0:1], consts[:, 0:1]))
            wait(o, "sync", "v", sm_end[i])
            wts[i] = dma(o, wtd.ap()[128 * i:128 * (i + 1), :],
                         wtb[:, p * 144:(p + 1) * 144])

        for i in range(NB):
            p4_a(i)
            if i >= 1:
                p4_b(i - 1)
        p4_b(NB - 1)
        o.emit()

        o = Ops()
        x2_store = 0
        for i in range(NB):
            if i > 0:
                wait(o, "sync", "v", mul_done)
                wait(o, "sync", "v", res_done)
            dw = dma(o, wtb[:, 0:144], wtd.ap()[128 * i:128 * (i + 1), :])
            dx = dma(o, xbuf[:, 0:C], xin.ap()[128 * i:128 * (i + 1), :])
            dk = 0
            for k in range(K9):
                dk = dma(o, xk9[:, C * k:C * (k + 1)],
                         ln1tm.ap()[128 * i + k:128 * i + k + 128, :])
            wait(o, "vector", "d", dk)
            wait(o, "vector", "d", dw)
            for k in range(K9):
                src = xk9[:, C * k:C * (k + 1)].rearrange("p (h d) -> p h d", h=H)
                wslice = wtb[:, 0:144].rearrange("p (h j) -> p h j", h=H)[:, :, k:k + 1]
                if k == 0:
                    op(o, "vector", lambda e, a=src, b=wslice: e.tensor_mul(
                        x2b[:, 0:C].rearrange("p (h d) -> p h d", h=H), a,
                        b.broadcast_to([128, H, HD])))
                else:
                    op(o, "vector", lambda e, a=src, b=wslice: e.tensor_mul(
                        tmp[:, 0:C].rearrange("p (h d) -> p h d", h=H), a,
                        b.broadcast_to([128, H, HD])))
                    fn = lambda e: e.tensor_add(x2b[:, 0:C], x2b[:, 0:C], tmp[:, 0:C])
                    if k == K9 - 1:
                        mul_done = ev(o, "vector", fn)
                    else:
                        op(o, "vector", fn)
            wait(o, "scalar", "v", mul_done)
            v1 = ev(o, "vector", lambda e: e.reduce_sum(SX(0), x2b[:, 0:C], AX.X))
            wait(o, "scalar", "v", v1)
            s1 = ev(o, "scalar", lambda e: e.activation(
                junk[:, :], x2b[:, 0:C], AF.Square, accum_out=SQ(0)))
            wait(o, "vector", "s", s1)
            op(o, "vector", lambda e: e.tensor_scalar_mul(MEAN(0), SX(0), 1.0 / C))
            op(o, "vector", lambda e: e.tensor_scalar_mul(NMEAN(0), SX(0), -1.0 / C))
            op(o, "vector", lambda e: e.drain())
            op(o, "vector", lambda e: e.tensor_mul(TMPV(0), MEAN(0), MEAN(0)))
            op(o, "vector", lambda e: e.tensor_scalar_mul(VAR(0), SQ(0), 1.0 / C))
            op(o, "vector", lambda e: e.drain())
            v2 = ev(o, "vector", lambda e: e.tensor_sub(VAR(0), VAR(0), TMPV(0)))
            wait(o, "scalar", "v", v2)
            s2 = ev(o, "scalar", lambda e: e.activation(
                RSTD(0), VAR(0), AF.Sqrt, bias=float(EPS), scale=1.0))
            wait(o, "vector", "s", s2)
            op(o, "vector", lambda e: e.reciprocal(RSTD(0), RSTD(0)))
            op(o, "vector", lambda e: e.drain())
            v3 = ev(o, "vector", lambda e: e.tensor_mul(NBIAS(0), NMEAN(0), RSTD(0)))
            wait(o, "scalar", "v", v3)
            s3 = ev(o, "scalar", lambda e: e.activation(
                tmp[:, 0:C], x2b[:, 0:C], AF.Identity, bias=NBIAS(0), scale=RSTD(0)))
            wait(o, "vector", "s", s3)
            res_done = ev(o, "vector", lambda e: e.tensor_add(x2b[:, C:2 * C], tmp[:, 0:C], xbuf[:, 0:C]))
            wait(o, "sync", "v", res_done)
            x2_store = dma(o, x2d.ap()[128 * i:128 * (i + 1), :], x2b[:, C:2 * C])
        o.emit()

        ln_phase(x2d, None, ln3fm, 1)

        o = Ops()
        NIT = NCH * 4
        xr_ev = [0] * NIT
        xk_ev = [0] * NIT
        st_xr = [0] * NIT
        st_xk = [0] * NIT
        it = 0
        for c in range(NCH):
            for hh in range(4):
                p = it & 1
                if it >= 2:
                    wait(o, "sync", "v", xk_ev[it - 2])
                dc = dma(o, fmc[:, p * (C + 1):(p + 1) * (C + 1)],
                         ln3fm.ap().rearrange("(c p) s -> p c s", p=128)
                         [:, c, C * hh:C * hh + C + 1])
                wait(o, "vector", "d", dc)
                dv = ev(o, "vector", lambda e, p2=p: e.tensor_sub(
                    diff[:, :], fmc[:, p2 * (C + 1) + 1:(p2 + 1) * (C + 1)],
                    fmc[:, p2 * (C + 1):p2 * (C + 1) + C]))
                if it >= 2:
                    wait(o, "vector", "d", st_xr[it - 2])
                xr_ev[it] = ev(o, "vector", lambda e, c2=c, p2=p: e.scalar_tensor_tensor(
                    xrb[:, p2 * C:(p2 + 1) * C], diff[:, :], mur_sb[:, c2:c2 + 1],
                    fmc[:, p2 * (C + 1):p2 * (C + 1) + C], ALU.mult, ALU.add))
                if it >= 2:
                    wait(o, "vector", "d", st_xk[it - 2])
                xk_ev[it] = ev(o, "vector", lambda e, c2=c, p2=p: e.scalar_tensor_tensor(
                    xkb[:, p2 * C:(p2 + 1) * C], diff[:, :], muk_sb[:, c2:c2 + 1],
                    fmc[:, p2 * (C + 1):p2 * (C + 1) + C], ALU.mult, ALU.add))
                wait(o, "sync", "v", xr_ev[it])
                st_xr[it] = dma(o, xrd.ap()[128 * c:128 * (c + 1), C * hh:C * (hh + 1)],
                                xrb[:, p * C:(p + 1) * C])
                wait(o, "sync", "v", xk_ev[it])
                st_xk[it] = dma(o, xkd.ap()[128 * c:128 * (c + 1), C * hh:C * (hh + 1)],
                                xkb[:, p * C:(p + 1) * C])
                it += 1
        o.emit()

        o = Ops()
        banks = (psA, psB, psC, psD)
        res_prev = [0] * 32
        out_prev = [0] * 32
        gt_last_v = 0
        v_mm_last = 0
        ln4_sq_prev = 0
        for ti in range(8):
            s0 = 512 * ti
            if ti > 0:
                wait(o, "sync", "t", v_mm_last)
                wait(o, "sync", "v", gt_last_v)
            dxr = dma(o, xr_sb[:, :].rearrange("p (c s) -> p c s", c=NCH),
                      xrd.ap().rearrange("(c p) s -> p c s", p=128)[:, :, s0:s0 + 512])
            dxk = dma(o, xk_sb[:, :].rearrange("p (c s) -> p c s", c=NCH),
                      xkd.ap().rearrange("(c p) s -> p c s", p=128)[:, :, s0:s0 + 512])
            wait(o, "tensor", "d", dxk)
            kk_evs = [0] * NCH
            kact_evs = [0] * NCH
            for d in range(NCH):
                ps = banks[d % 4]
                if ti > 0 and d < 4:
                    wait(o, "tensor", "v", gt_last_v)
                if d >= 4:
                    wait(o, "tensor", "v", kk_evs[d - 4])
                for c in range(NCH):
                    fn = lambda e, c2=c, d2=d, p2=ps: e.matmul(
                        p2[:, :], wk_sb[:, c2 * C + 128 * d2: c2 * C + 128 * (d2 + 1)],
                        xk_sb[:, 512 * c2:512 * (c2 + 1)],
                        start=(c2 == 0), stop=(c2 == NCH - 1))
                    if c == NCH - 1:
                        mmk = ev(o, "tensor", fn)
                    else:
                        op(o, "tensor", fn)
                wait(o, "vector", "t", mmk)
                pk = d & 1
                if d >= 2:
                    wait(o, "vector", "s", kact_evs[d - 2])
                kk_evs[d] = ev(o, "vector", lambda e, p2=ps, pk2=pk: e.tensor_scalar_max(
                    kkb[:, pk2 * 512:(pk2 + 1) * 512], p2[:, :], 0.0))
                wait(o, "scalar", "v", kk_evs[d])
                kact_evs[d] = ev(o, "scalar", lambda e, d2=d, pk2=pk: e.activation(
                    kact[:, 512 * d2:512 * (d2 + 1)], kkb[:, pk2 * 512:(pk2 + 1) * 512],
                    AF.Square))
            wait(o, "tensor", "d", dxr)
            r_evs = [0] * 8
            for u in range(4):
                for hf in range(2):
                    idx = 2 * u + hf
                    ps = banks[idx % 4]
                    if idx < 4:
                        wait(o, "tensor", "v", kk_evs[4 + idx])
                    else:
                        wait(o, "tensor", "s", r_evs[idx - 4])
                    for c in range(NCH):
                        fn = lambda e, c2=c, u2=u, h2=hf, p2=ps: e.matmul(
                            p2[:, :], xr_sb[:, 512 * c2 + 128 * u2: 512 * c2 + 128 * (u2 + 1)],
                            wr_sb[:, c2 * C + 512 * h2: c2 * C + 512 * (h2 + 1)],
                            start=(c2 == 0), stop=(c2 == NCH - 1))
                        if c == NCH - 1:
                            mmr = ev(o, "tensor", fn)
                        else:
                            op(o, "tensor", fn)
                    wait(o, "scalar", "t", mmr)
                    r_evs[idx] = ev(o, "scalar", lambda e, u2=u, h2=hf, p2=ps: e.activation(
                        r_sb[:, C * u2 + 512 * h2: C * u2 + 512 * (h2 + 1)],
                        p2[:, :], AF.Sigmoid, bias=0.0, scale=1.0))
            v_evs = [0] * 8
            for u in range(4):
                pu = 0
                g = ti * 4 + u
                if g >= 1:
                    wait(o, "sync", "v", res_prev[g - 1])
                dx2u = dma(o, x2t[:, :],
                           x2d.ap()[s0 + 128 * u:s0 + 128 * (u + 1), :])
                for hf in range(2):
                    idx = 2 * u + hf
                    ps = banks[idx % 4]
                    if idx == 0:
                        wait(o, "tensor", "s", kact_evs[NCH - 1])
                    if idx < 4:
                        wait(o, "tensor", "s", r_evs[4 + idx])
                    else:
                        wait(o, "tensor", "v", v_evs[idx - 4])
                    for d in range(NCH):
                        fn = lambda e, d2=d, u2=u, h2=hf, p2=ps: e.matmul(
                            p2[:, :], kact[:, 512 * d2 + 128 * u2: 512 * d2 + 128 * (u2 + 1)],
                            wv_sb[:, d2 * C + 512 * h2: d2 * C + 512 * (h2 + 1)],
                            start=(d2 == 0), stop=(d2 == NCH - 1))
                        if d == NCH - 1:
                            mmv = ev(o, "tensor", fn)
                        else:
                            op(o, "tensor", fn)
                    wait(o, "vector", "t", mmv)
                    if g > 0 or hf > 0:
                        wait(o, "vector", "s", ln4_sq_prev)
                    v_evs[idx] = ev(o, "vector", lambda e, u2=u, h2=hf, p2=ps: e.tensor_mul(
                        ffn[:, C * u2 + 512 * h2: C * u2 + 512 * (h2 + 1)],
                        p2[:, :], r_sb[:, C * u2 + 512 * h2: C * u2 + 512 * (h2 + 1)]))
                    v_mm_last = mmv
                gt_last_v = v_evs[2 * u + 1]
                pp = g & 1
                fsl = ffn[:, C * u:C * (u + 1)]
                v1 = ev(o, "vector", lambda e, a=fsl, p2=pp: e.reduce_sum(SX(p2), a, AX.X))
                wait(o, "scalar", "v", v1)
                ln4_sq_prev = ev(o, "scalar", lambda e, a=fsl, p2=pp: e.activation(
                    junk[:, :], a, AF.Square, accum_out=SQ(p2)))
                wait(o, "vector", "s", ln4_sq_prev)
                v3 = small_chain(o, pp)
                wait(o, "scalar", "v", v3)
                if g >= 2:
                    wait(o, "scalar", "v", res_prev[g - 2])
                s3 = ev(o, "scalar", lambda e, a=fsl, p2=pp: e.activation(
                    tmp[:, p2 * C:(p2 + 1) * C], a, AF.Identity,
                    bias=NBIAS(p2), scale=RSTD(p2)))
                wait(o, "vector", "s", s3)
                wait(o, "vector", "d", dx2u)
                if g >= 1:
                    wait(o, "vector", "d", out_prev[g - 1])
                res_prev[g] = ev(o, "vector", lambda e, pu2=pu, p2=pp: e.tensor_add(
                    outb[:, :], tmp[:, p2 * C:(p2 + 1) * C],
                    x2t[:, pu2 * C:(pu2 + 1) * C]))
                wait(o, "sync", "v", res_prev[g])
                out_prev[g] = dma(
                    o, out_h.ap()[s0 + 128 * u:s0 + 128 * (u + 1), :],
                    outb[:, :])
        wait(o, "sync", "d", ct["d"])
        wait(o, "sync", "g", ct["g"])
        o.emit()

    return nc


_NC_CACHE = {}


def _get_nc(debug=False):
    if debug not in _NC_CACHE:
        _NC_CACHE[debug] = build_nc(debug)
    return _NC_CACHE[debug]


def kernel(L0_attn_query, L0_attn_conv_weight, L0_ffn_W_r, L0_ffn_W_k,
           L0_ffn_W_v, L0_ffn_mu_r, L0_ffn_mu_k):
    x = np.ascontiguousarray(np.asarray(L0_attn_query, dtype=np.float32))
    cw = np.asarray(L0_attn_conv_weight, dtype=np.float32)
    wr = np.asarray(L0_ffn_W_r, dtype=np.float32)
    wk = np.asarray(L0_ffn_W_k, dtype=np.float32)
    wv = np.asarray(L0_ffn_W_v, dtype=np.float32)
    mur = np.asarray(L0_ffn_mu_r, dtype=np.float32)
    muk = np.asarray(L0_ffn_mu_k, dtype=np.float32)

    wrT = np.ascontiguousarray(wr.T)
    wkT = np.ascontiguousarray(wk.T)
    wvT = np.ascontiguousarray(wv.T)
    cwq = np.zeros((NCH, 128, 162), dtype=np.float32)
    cwr = cw.reshape(H, 9, HD, K9)
    for cc in range(NCH):
        for hl in range(2):
            h = 2 * cc + hl
            blk = np.transpose(cwr[h], (1, 2, 0))
            cols = (np.arange(K9)[:, None] * 18 + hl * 9 + np.arange(9)[None, :])
            cwq[cc, 64 * hl:64 * (hl + 1), cols.reshape(-1)] = blk.reshape(HD, 81).T
    mur8 = np.ascontiguousarray(mur.reshape(NCH, 128).T)
    muk8 = np.ascontiguousarray(muk.reshape(NCH, 128).T)
    idb = np.eye(128, dtype=np.float32).astype(ml_dtypes.bfloat16)

    nc = _get_nc()
    in_maps = []
    for b in range(NCORES):
        in_maps.append({
            "xin": np.ascontiguousarray(x[b]),
            "wrT": wrT, "wkT": wkT, "wvT": wvT,
            "cwq": cwq, "mur": mur8, "muk": muk8, "idbh": idb,
        })
    res = run_bass_kernel_spmd(nc, in_maps, core_ids=list(range(NCORES)))
    out = np.stack([np.asarray(r["out"]) for r in res.results], axis=0)
    return out.astype(np.float32)


# version 24
# speedup vs baseline: 1.1276x; 1.0676x over previous
import contextlib
import numpy as np
import ml_dtypes
import concourse.bass as bass
import concourse.mybir as mybir
from concourse.bass_utils import run_bass_kernel_spmd

F32 = mybir.dt.float32
BF16 = mybir.dt.bfloat16
AF = mybir.ActivationFunctionType
ALU = mybir.AluOpType
AX = mybir.AxisListType

S = 4096
C = 1024
NB = S // 128
NCH = 8
H = 16
HD = 64
K9 = 9
EPS = 1e-5
NCORES = 8


def build_nc(debug=False):
    nc = bass.Bass("TRN2", target_bir_lowering=False)

    xin = nc.dram_tensor("xin", [S, C], F32, kind="ExternalInput")
    wrT_h = nc.dram_tensor("wrT", [C, C], F32, kind="ExternalInput")
    wkT_h = nc.dram_tensor("wkT", [C, C], F32, kind="ExternalInput")
    wvT_h = nc.dram_tensor("wvT", [C, C], F32, kind="ExternalInput")
    cwq_h = nc.dram_tensor("cwq", [NCH, 128, 162], F32, kind="ExternalInput")
    mur_h = nc.dram_tensor("mur", [128, NCH], F32, kind="ExternalInput")
    muk_h = nc.dram_tensor("muk", [128, NCH], F32, kind="ExternalInput")
    idb_h = nc.dram_tensor("idbh", [128, 128], BF16, kind="ExternalInput")
    out_h = nc.dram_tensor("out", [S, C], F32, kind="ExternalOutput")

    kd = "ExternalOutput" if debug else "Internal"
    ln1tm = nc.dram_tensor("ln1tm", [S + 8, C], BF16, kind=kd)
    ln1fm = nc.dram_tensor("ln1fm", [C, S], BF16, kind=kd)
    qdr = nc.dram_tensor("qdr", [S + 8, 9 * 144], F32, kind=kd)
    wtd = nc.dram_tensor("wtd", [S, 144], BF16, kind=kd)
    x2d = nc.dram_tensor("x2d", [S, C], F32, kind=kd)
    ln3fm = nc.dram_tensor("ln3fm", [C, S + 1], BF16, kind=kd)
    xrd = nc.dram_tensor("xrd", [C, S], BF16, kind=kd)
    xkd = nc.dram_tensor("xkd", [C, S], BF16, kind=kd)
    tmpdbg = nc.dram_tensor("tmpdbg", [S, C], F32, kind=kd)
    ydbg = nc.dram_tensor("ydbg", [S, C], F32, kind=kd)
    svdbg = nc.dram_tensor("svdbg", [S, 20], F32, kind=kd)

    with contextlib.ExitStack() as st:
        def sb(name, shape, dt=F32):
            return st.enter_context(nc.sbuf_tensor(name, shape, dt))

        wr_sb = sb("wr_sb", [128, NCH * C], BF16)
        wk_sb = sb("wk_sb", [128, NCH * C], BF16)
        wv_sb = sb("wv_sb", [128, NCH * C], BF16)
        cwq_sb = sb("cwq_sb", [128, NCH * 162], BF16)
        mur_sb = sb("mur_sb", [128, NCH])
        muk_sb = sb("muk_sb", [128, NCH])
        idb = sb("idb", [128, 128], BF16)
        consts = sb("consts", [128, 4])

        xbuf = sb("xbuf", [128, 2 * C])
        lnb = sb("lnb", [128, 2 * C], BF16)
        fmst = sb("fmst", [128, 2 * C], BF16)
        junk = sb("junk", [128, C])
        sv = sb("sv", [128, 20])
        fmwin = sb("fmwin", [128, 2 * NCH * 128], BF16)
        qstage = sb("qstage", [128, 2 * 1296])
        acc = sb("acc", [128, 2 * 144])
        mx = sb("mx", [128, 2 * H])
        zsum = sb("zsum", [128, 2 * H])
        wtb = sb("wtb", [128, 2 * 144], BF16)
        xk9 = sb("xk9", [128, 9 * C], BF16)
        acc2 = sb("acc2", [128, C])
        ptmp = sb("ptmp", [128, C])
        acc2d = sb("acc2d", [128, C])
        tmp = sb("tmp", [128, 2 * C])
        x2b = sb("x2b", [128, 2 * C])
        fmc = sb("fmc", [128, 2 * (C + 1)], BF16)
        diff = sb("diff", [128, C], BF16)
        xrb = sb("xrb", [128, 2 * C], BF16)
        xkb = sb("xkb", [128, 2 * C], BF16)
        xr_sb = sb("xr_sb", [128, NCH * 512], BF16)
        xk_sb = sb("xk_sb", [128, NCH * 512], BF16)
        kact = sb("kact", [128, NCH * 512], BF16)
        kkb = sb("kkb", [128, 2 * 512])
        r_sb = sb("r_sb", [128, 4 * C], BF16)
        ffn = sb("ffn", [128, 4 * C])
        x2t = sb("x2t", [128, C])
        outb = sb("outb", [128, C])
        dum = sb("dum", [128, 4])

        psQa = st.enter_context(nc.psum_tensor("psQa", [128, 162], F32))
        psQb = st.enter_context(nc.psum_tensor("psQb", [128, 162], F32))
        psT0 = st.enter_context(nc.psum_tensor("psT0", [128, C], BF16))
        psT1 = st.enter_context(nc.psum_tensor("psT1", [128, C], BF16))
        psA = st.enter_context(nc.psum_tensor("psA", [128, 512], F32))
        psB = st.enter_context(nc.psum_tensor("psB", [128, 512], F32))
        psC = st.enter_context(nc.psum_tensor("psC", [128, 512], F32))
        psD = st.enter_context(nc.psum_tensor("psD", [128, 512], F32))

        dsm = st.enter_context(nc.semaphore("dsm"))
        gsm = st.enter_context(nc.semaphore("gsm"))
        vsm = st.enter_context(nc.semaphore("vsm"))
        ssm = st.enter_context(nc.semaphore("ssm"))
        tsm = st.enter_context(nc.semaphore("tsm"))
        psm = st.enter_context(nc.semaphore("psm"))

        nc.const_aps.register(F32, 0.0, consts[:, 0:1])
        nc.const_aps.register(F32, 1.0, consts[:, 1:2])
        nc.const_aps.register(F32, float(EPS), consts[:, 2:3])

        ct = dict(d=0, g=0, v=0, s=0, t=0, p=0)
        SEMS = dict(d=dsm, g=gsm, v=vsm, s=ssm, t=tsm, p=psm)
        EKEY = dict(vector="v", scalar="s", tensor="t", gpsimd="p")

        class Ops:
            def __init__(self):
                self.l = dict(sync=[], vector=[], scalar=[], tensor=[], gpsimd=[])

            def add(self, eng, fn):
                self.l[eng].append(fn)

            def emit(self):
                with nc.Block() as blk:
                    for eng, dec in (("sync", blk.sync), ("vector", blk.vector),
                                     ("scalar", blk.scalar), ("tensor", blk.tensor),
                                     ("gpsimd", blk.gpsimd)):
                        if self.l[eng]:
                            dec(lambda e, L=self.l[eng]: [f(e) for f in L])

        def dma(ops, dst, src, eng="sync", **kw):
            key = "g" if eng == "gpsimd" else "d"
            sem = SEMS[key]
            ct[key] += 16
            ops.add(eng, lambda e, a=dst, b=src, s2=sem, k2=kw:
                    e.dma_start(a, b, **k2).then_inc(s2, 16))
            return ct[key]

        def wait(ops, eng, sem_key, val):
            if val:
                ops.add(eng, lambda e, s2=SEMS[sem_key], v2=val: e.wait_ge(s2, v2))

        def ev(ops, eng, fn):
            key = EKEY[eng]
            sem = SEMS[key]
            ct[key] += 1
            ops.add(eng, lambda e, f=fn, s2=sem: f(e).then_inc(s2, 1))
            return ct[key]

        def op(ops, eng, fn):
            ops.add(eng, fn)

        def SX(p): return sv[:, 10 * p + 0: 10 * p + 1]
        def SQ(p): return sv[:, 10 * p + 1: 10 * p + 2]
        def MEAN(p): return sv[:, 10 * p + 2: 10 * p + 3]
        def NMEAN(p): return sv[:, 10 * p + 3: 10 * p + 4]
        def VAR(p): return sv[:, 10 * p + 4: 10 * p + 5]
        def RSTD(p): return sv[:, 10 * p + 5: 10 * p + 6]
        def NBIAS(p): return sv[:, 10 * p + 6: 10 * p + 7]
        def TMPV(p): return sv[:, 10 * p + 7: 10 * p + 8]
        def SXB(p): return sv[:, 10 * p + 8: 10 * p + 9]
        def SQB(p): return sv[:, 10 * p + 9: 10 * p + 10]

        def small_chain(o, p, extra_banks=False):
            if extra_banks:
                op(o, "vector", lambda e: e.tensor_add(SX(p), SX(p), SXB(p)))
                op(o, "vector", lambda e: e.tensor_add(SQ(p), SQ(p), SQB(p)))
                op(o, "vector", lambda e: e.drain())
            op(o, "vector", lambda e: e.tensor_scalar_mul(MEAN(p), SX(p), 1.0 / C))
            op(o, "vector", lambda e: e.tensor_scalar_mul(NMEAN(p), SX(p), -1.0 / C))
            op(o, "vector", lambda e: e.drain())
            op(o, "vector", lambda e: e.tensor_mul(TMPV(p), MEAN(p), MEAN(p)))
            op(o, "vector", lambda e: e.tensor_scalar_mul(VAR(p), SQ(p), 1.0 / C))
            op(o, "vector", lambda e: e.drain())
            op(o, "vector", lambda e: e.tensor_sub(VAR(p), VAR(p), TMPV(p)))
            v2 = ev(o, "vector", lambda e: e.tensor_copy(dum[:, 0:1], consts[:, 0:1]))
            wait(o, "scalar", "v", v2)
            op(o, "scalar", lambda e: e.activation(
                RSTD(p), VAR(p), AF.Sqrt, bias=float(EPS), scale=1.0))
            s2 = ev(o, "scalar", lambda e: e.copy(dum[:, 2:3], consts[:, 0:1]))
            wait(o, "vector", "s", s2)
            op(o, "vector", lambda e: e.reciprocal(RSTD(p), RSTD(p)))
            op(o, "vector", lambda e: e.drain())
            op(o, "vector", lambda e: e.tensor_mul(NBIAS(p), NMEAN(p), RSTD(p)))
            v3 = ev(o, "vector", lambda e: e.tensor_copy(dum[:, 0:1], consts[:, 0:1]))
            return v3

        o = Ops()
        op(o, "vector", lambda e: e.memset(consts[:, :], 0.0))
        op(o, "vector", lambda e: e.memset(consts[:, 1:2], 1.0))
        op(o, "vector", lambda e: e.memset(consts[:, 2:3], float(EPS)))
        op(o, "vector", lambda e: e.memset(qstage[:, :], 0.0))
        zdone = ev(o, "vector", lambda e: e.memset(lnb[:, :], 0.0))
        dma(o, idb[:, :], idb_h.ap())
        wait(o, "sync", "v", zdone)
        dma(o, ln1tm.ap()[0:4, :], lnb[0:4, 0:C])
        dma(o, ln1tm.ap()[S + 4:S + 8, :], lnb[0:4, 0:C])
        dma(o, qdr.ap()[0:4, :], qstage[0:4, 0:1296])
        pad_done = dma(o, qdr.ap()[S + 4:S + 8, :], qstage[0:4, 0:1296])
        dma(o, ln3fm.ap().rearrange("(c p) s -> p c s", p=128)[:, :, 0:1],
            lnb[:, 0:NCH][:, :, None], allow_slow_non_contiguous=True)
        cast_done = 0
        for wsrc, wdst in ((wrT_h, wr_sb), (wkT_h, wk_sb), (wvT_h, wv_sb)):
            for c in range(NCH):
                if cast_done:
                    wait(o, "sync", "g", cast_done)
                dl = dma(o, junk[:, :], wsrc.ap()[128 * c:128 * (c + 1), :])
                wait(o, "gpsimd", "d", dl)
                cast_done = dma(o, wdst[:, c * C:(c + 1) * C], junk[:, :], eng="gpsimd")
        wait(o, "sync", "d", pad_done)
        dl = dma(o, qstage[:, 0:NCH * 162].rearrange("p (c n) -> p c n", c=NCH),
                 cwq_h.ap().rearrange("c p n -> p c n"))
        wait(o, "gpsimd", "d", dl)
        dma(o, cwq_sb[:, :], qstage[:, 0:NCH * 162], eng="gpsimd")
        dma(o, mur_sb[:, :], mur_h.ap())
        dma(o, muk_sb[:, :], muk_h.ap())
        o.emit()

        def ln_phase(src_h, tm_dst, fm_dst, fm_col0):
            o = Ops()
            dx = [0] * NB
            sq = [0] * NB
            red = [0] * NB
            apl = [0] * NB
            tr = [0] * NB
            evc = [0] * NB
            tms = [0] * NB
            fms = [0] * NB
            v3a = [0] * NB

            def stage_l(i):
                p = i & 1
                if i >= 2:
                    wait(o, "sync", "s", apl[i - 2])
                dx[i] = dma(o, xbuf[:, p * C:(p + 1) * C],
                            src_h.ap()[128 * i:128 * (i + 1), :])

            def stage_a(i):
                p = i & 1
                wait(o, "vector", "d", dx[i])
                red[i] = ev(o, "vector", lambda e, p2=p: e.reduce_sum(
                    SX(p2), xbuf[:, p2 * C:(p2 + 1) * C], AX.X))
                wait(o, "scalar", "d", dx[i])
                wait(o, "scalar", "v", red[i])
                if i >= 2:
                    wait(o, "scalar", "v", v3a[i - 2])
                op(o, "scalar", lambda e, p2=p: e.activation(
                    junk[:, :], xbuf[:, p2 * C:(p2 + 1) * C], AF.Square,
                    accum_out=SQ(p2)))
                sq[i] = ev(o, "scalar", lambda e: e.copy(dum[:, 2:3], consts[:, 0:1]))

            def stage_b(i):
                p = i & 1
                wait(o, "vector", "s", sq[i])
                v3 = small_chain(o, p)
                v3a[i] = v3
                wait(o, "scalar", "v", v3)
                if i >= 2:
                    wait(o, "scalar", "t", tr[i - 2])
                    if tm_dst is not None:
                        wait(o, "scalar", "d", tms[i - 2])
                op(o, "scalar", lambda e, p2=p: e.activation(
                    lnb[:, p2 * C:(p2 + 1) * C], xbuf[:, p2 * C:(p2 + 1) * C],
                    AF.Identity, bias=NBIAS(p2), scale=RSTD(p2)))
                apl[i] = ev(o, "scalar", lambda e: e.copy(dum[:, 2:3], consts[:, 0:1]))
                if tm_dst is not None:
                    wait(o, "sync", "s", apl[i])
                    tms[i] = dma(o, tm_dst.ap()[4 + 128 * i:4 + 128 * (i + 1), :],
                                 lnb[:, p * C:(p + 1) * C])
                wait(o, "tensor", "s", apl[i])
                if i >= 2:
                    wait(o, "tensor", "v", evc[i - 2])
                pst = psT0 if p == 0 else psT1
                for c in range(NCH):
                    fn = lambda e, c2=c, p2=p, ps=pst: e.transpose(
                        ps[:, 128 * c2:128 * (c2 + 1)],
                        lnb[:, p2 * C + 128 * c2:p2 * C + 128 * (c2 + 1)], idb[:, :])
                    if c == NCH - 1:
                        tr[i] = ev(o, "tensor", fn)
                    else:
                        op(o, "tensor", fn)
                wait(o, "vector", "t", tr[i])
                if i >= 2:
                    wait(o, "vector", "d", fms[i - 2])
                evc[i] = ev(o, "vector", lambda e, p2=p, ps=pst: e.tensor_copy(
                    fmst[:, p2 * C:(p2 + 1) * C], ps[:, :]))
                wait(o, "sync", "v", evc[i])
                fms[i] = dma(
                    o, fm_dst.ap().rearrange("(c p) s -> p c s", p=128)
                    [:, :, fm_col0 + 128 * i:fm_col0 + 128 * (i + 1)],
                    fmst[:, p * C:(p + 1) * C].rearrange("p (c s) -> p c s", c=NCH))

            for i in range(NB + 2):
                if i >= 2:
                    stage_b(i - 2)
                if i < NB:
                    stage_l(i)
                if 1 <= i <= NB:
                    stage_a(i - 1)
            o.emit()

        ln_phase(xin, ln1tm, ln1fm, 0)

        o = Ops()
        mml = {}
        evq = {}
        qst = [0] * NB
        for i in range(NB):
            p = i & 1
            if i >= 2:
                wait(o, "sync", "t", mml[(i - 2, NCH - 1)])
            dw = dma(o, fmwin[:, p * NCH * 128:(p + 1) * NCH * 128]
                     .rearrange("p (c s) -> p c s", c=NCH),
                     ln1fm.ap().rearrange("(c p) s -> p c s", p=128)
                     [:, :, 128 * i:128 * (i + 1)])
            wait(o, "tensor", "d", dw)
            for c in range(NCH):
                ps = psQa if (c % 2 == 0) else psQb
                if i > 0 or c >= 2:
                    prev = evq[(i, c - 2)] if c >= 2 else evq[(i - 1, NCH - 2 + c)]
                    wait(o, "tensor", "s", prev)
                mml[(i, c)] = ev(o, "tensor", lambda e, c2=c, p2=p, ps2=ps: e.matmul(
                    ps2[:, :],
                    fmwin[:, p2 * NCH * 128 + 128 * c2:p2 * NCH * 128 + 128 * (c2 + 1)],
                    cwq_sb[:, 162 * c2:162 * (c2 + 1)], start=True, stop=True))
                wait(o, "scalar", "t", mml[(i, c)])
                if c == 0 and i >= 2:
                    wait(o, "scalar", "d", qst[i - 2])
                evq[(i, c)] = ev(o, "scalar", lambda e, c2=c, p2=p, ps2=ps: e.activation(
                    qstage[:, p2 * 1296:(p2 + 1) * 1296]
                    .rearrange("p (t h j) -> p t h j", t=9, h=H)[:, :, 2 * c2:2 * c2 + 2, :],
                    ps2[:, :].rearrange("p (t hl j) -> p t hl j", t=9, hl=2),
                    AF.Identity, bias=0.0, scale=1.0))
            wait(o, "sync", "s", evq[(i, NCH - 1)])
            qst[i] = dma(o, qdr.ap()[4 + 128 * i:4 + 128 * (i + 1), :],
                         qstage[:, p * 1296:(p + 1) * 1296])
        o.emit()

        o = Ops()
        addv = [0] * NB
        sm_end = [0] * NB
        wts = [0] * NB

        dq4 = [0] * NB

        def p4_l(i):
            p = i & 1
            if i >= 2:
                wait(o, "sync", "p", addv[i - 2])
            for t in range(K9):
                dq4[i] = dma(o, qstage[:, p * 1296 + 144 * t:p * 1296 + 144 * (t + 1)],
                             qdr.ap()[128 * i + t:128 * i + t + 128, 144 * t:144 * (t + 1)])

        def p4_a(i):
            p = i & 1
            wait(o, "gpsimd", "d", dq4[i])
            if i >= 2:
                wait(o, "gpsimd", "v", sm_end[i - 2])
            q0 = p * 1296
            op(o, "gpsimd", lambda e, p2=p, q2=q0: e.tensor_add(
                acc[:, p2 * 144:(p2 + 1) * 144], qstage[:, q2:q2 + 144],
                qstage[:, q2 + 144:q2 + 288]))
            for t in range(2, K9):
                op(o, "gpsimd", lambda e, p2=p, q2=q0, t2=t: e.tensor_add(
                    acc[:, p2 * 144:(p2 + 1) * 144], acc[:, p2 * 144:(p2 + 1) * 144],
                    qstage[:, q2 + 144 * t2:q2 + 144 * (t2 + 1)]))
            addv[i] = ev(o, "gpsimd", lambda e: e.tensor_copy(dum[:, 1:2], consts[:, 0:1]))

        def p4_b(i):
            p = i & 1
            av = acc[:, p * 144:(p + 1) * 144].rearrange("p (h j) -> p h j", h=H)
            wait(o, "vector", "p", addv[i])
            op(o, "vector", lambda e, a=av, p2=p: e.reduce_max(
                mx[:, p2 * H:(p2 + 1) * H][:, :, None], a, AX.X))
            op(o, "vector", lambda e: e.drain())
            v1 = ev(o, "vector", lambda e, a=av, p2=p: e.tensor_sub(
                a, a, mx[:, p2 * H:(p2 + 1) * H][:, :, None].broadcast_to([128, H, 9])))
            wait(o, "scalar", "v", v1)
            op(o, "scalar", lambda e, p2=p: e.activation(
                acc[:, p2 * 144:(p2 + 1) * 144], acc[:, p2 * 144:(p2 + 1) * 144],
                AF.Exp, bias=0.0, scale=1.0))
            se = ev(o, "scalar", lambda e: e.copy(dum[:, 2:3], consts[:, 0:1]))
            wait(o, "vector", "s", se)
            op(o, "vector", lambda e, a=av, p2=p: e.reduce_sum(
                zsum[:, p2 * H:(p2 + 1) * H][:, :, None], a, AX.X))
            op(o, "vector", lambda e: e.drain())
            op(o, "vector", lambda e, p2=p: e.reciprocal(
                zsum[:, p2 * H:(p2 + 1) * H], zsum[:, p2 * H:(p2 + 1) * H]))
            op(o, "vector", lambda e: e.drain())
            if i >= 2:
                wait(o, "vector", "d", wts[i - 2])
            op(o, "vector", lambda e, a=av, p2=p: e.tensor_mul(
                wtb[:, p2 * 144:(p2 + 1) * 144].rearrange("p (h j) -> p h j", h=H),
                a, zsum[:, p2 * H:(p2 + 1) * H][:, :, None].broadcast_to([128, H, 9])))
            sm_end[i] = ev(o, "vector", lambda e: e.tensor_copy(dum[:, 0:1], consts[:, 0:1]))
            wait(o, "sync", "v", sm_end[i])
            wts[i] = dma(o, wtd.ap()[128 * i:128 * (i + 1), :],
                         wtb[:, p * 144:(p + 1) * 144])

        for i in range(NB + 2):
            if i >= 2:
                p4_b(i - 2)
            if i < NB:
                p4_l(i)
            if 1 <= i <= NB:
                p4_a(i - 1)
        o.emit()

        o = Ops()
        x2_store = 0
        for i in range(NB):
            if i > 0:
                wait(o, "sync", "v", mul_done)
                wait(o, "sync", "v", res_done)
            dw = dma(o, wtb[:, 0:144], wtd.ap()[128 * i:128 * (i + 1), :])
            dx = dma(o, xbuf[:, 0:C], xin.ap()[128 * i:128 * (i + 1), :])
            dk = 0
            for k in range(K9):
                dk = dma(o, xk9[:, C * k:C * (k + 1)],
                         ln1tm.ap()[128 * i + k:128 * i + k + 128, :])
            wait(o, "vector", "d", dk)
            wait(o, "vector", "d", dw)
            for k in range(K9):
                src = xk9[:, C * k:C * (k + 1)].rearrange("p (h d) -> p h d", h=H)
                wslice = wtb[:, 0:144].rearrange("p (h j) -> p h j", h=H)[:, :, k:k + 1]
                if k == 0:
                    op(o, "vector", lambda e, a=src, b=wslice: e.tensor_mul(
                        x2b[:, 0:C].rearrange("p (h d) -> p h d", h=H), a,
                        b.broadcast_to([128, H, HD])))
                else:
                    op(o, "vector", lambda e, a=src, b=wslice: e.tensor_mul(
                        tmp[:, 0:C].rearrange("p (h d) -> p h d", h=H), a,
                        b.broadcast_to([128, H, HD])))
                    fn = lambda e: e.tensor_add(x2b[:, 0:C], x2b[:, 0:C], tmp[:, 0:C])
                    if k == K9 - 1:
                        mul_done = ev(o, "vector", fn)
                    else:
                        op(o, "vector", fn)
            wait(o, "scalar", "v", mul_done)
            v1 = ev(o, "vector", lambda e: e.reduce_sum(SX(0), x2b[:, 0:C], AX.X))
            wait(o, "scalar", "v", v1)
            s1 = ev(o, "scalar", lambda e: e.activation(
                junk[:, :], x2b[:, 0:C], AF.Square, accum_out=SQ(0)))
            wait(o, "vector", "s", s1)
            op(o, "vector", lambda e: e.tensor_scalar_mul(MEAN(0), SX(0), 1.0 / C))
            op(o, "vector", lambda e: e.tensor_scalar_mul(NMEAN(0), SX(0), -1.0 / C))
            op(o, "vector", lambda e: e.drain())
            op(o, "vector", lambda e: e.tensor_mul(TMPV(0), MEAN(0), MEAN(0)))
            op(o, "vector", lambda e: e.tensor_scalar_mul(VAR(0), SQ(0), 1.0 / C))
            op(o, "vector", lambda e: e.drain())
            v2 = ev(o, "vector", lambda e: e.tensor_sub(VAR(0), VAR(0), TMPV(0)))
            wait(o, "scalar", "v", v2)
            s2 = ev(o, "scalar", lambda e: e.activation(
                RSTD(0), VAR(0), AF.Sqrt, bias=float(EPS), scale=1.0))
            wait(o, "vector", "s", s2)
            op(o, "vector", lambda e: e.reciprocal(RSTD(0), RSTD(0)))
            op(o, "vector", lambda e: e.drain())
            v3 = ev(o, "vector", lambda e: e.tensor_mul(NBIAS(0), NMEAN(0), RSTD(0)))
            wait(o, "scalar", "v", v3)
            s3 = ev(o, "scalar", lambda e: e.activation(
                tmp[:, 0:C], x2b[:, 0:C], AF.Identity, bias=NBIAS(0), scale=RSTD(0)))
            wait(o, "vector", "s", s3)
            res_done = ev(o, "vector", lambda e: e.tensor_add(x2b[:, C:2 * C], tmp[:, 0:C], xbuf[:, 0:C]))
            wait(o, "sync", "v", res_done)
            x2_store = dma(o, x2d.ap()[128 * i:128 * (i + 1), :], x2b[:, C:2 * C])
        o.emit()

        ln_phase(x2d, None, ln3fm, 1)

        o = Ops()
        NIT = NCH * 4
        xr_ev = [0] * NIT
        xk_ev = [0] * NIT
        st_xr = [0] * NIT
        st_xk = [0] * NIT
        it = 0
        for c in range(NCH):
            for hh in range(4):
                p = it & 1
                if it >= 2:
                    wait(o, "sync", "v", xk_ev[it - 2])
                dc = dma(o, fmc[:, p * (C + 1):(p + 1) * (C + 1)],
                         ln3fm.ap().rearrange("(c p) s -> p c s", p=128)
                         [:, c, C * hh:C * hh + C + 1])
                wait(o, "vector", "d", dc)
                dv = ev(o, "vector", lambda e, p2=p: e.tensor_sub(
                    diff[:, :], fmc[:, p2 * (C + 1) + 1:(p2 + 1) * (C + 1)],
                    fmc[:, p2 * (C + 1):p2 * (C + 1) + C]))
                if it >= 2:
                    wait(o, "vector", "d", st_xr[it - 2])
                xr_ev[it] = ev(o, "vector", lambda e, c2=c, p2=p: e.scalar_tensor_tensor(
                    xrb[:, p2 * C:(p2 + 1) * C], diff[:, :], mur_sb[:, c2:c2 + 1],
                    fmc[:, p2 * (C + 1):p2 * (C + 1) + C], ALU.mult, ALU.add))
                if it >= 2:
                    wait(o, "vector", "d", st_xk[it - 2])
                xk_ev[it] = ev(o, "vector", lambda e, c2=c, p2=p: e.scalar_tensor_tensor(
                    xkb[:, p2 * C:(p2 + 1) * C], diff[:, :], muk_sb[:, c2:c2 + 1],
                    fmc[:, p2 * (C + 1):p2 * (C + 1) + C], ALU.mult, ALU.add))
                wait(o, "sync", "v", xr_ev[it])
                st_xr[it] = dma(o, xrd.ap()[128 * c:128 * (c + 1), C * hh:C * (hh + 1)],
                                xrb[:, p * C:(p + 1) * C])
                wait(o, "sync", "v", xk_ev[it])
                st_xk[it] = dma(o, xkd.ap()[128 * c:128 * (c + 1), C * hh:C * (hh + 1)],
                                xkb[:, p * C:(p + 1) * C])
                it += 1
        o.emit()

        o = Ops()
        banks = (psA, psB, psC, psD)
        res_prev = [0] * 32
        out_prev = [0] * 32
        gt_last_v = 0
        v_mm_last = 0
        ln4_sq_prev = 0
        for ti in range(8):
            s0 = 512 * ti
            if ti > 0:
                wait(o, "sync", "t", v_mm_last)
                wait(o, "sync", "v", gt_last_v)
            dxr = dma(o, xr_sb[:, :].rearrange("p (c s) -> p c s", c=NCH),
                      xrd.ap().rearrange("(c p) s -> p c s", p=128)[:, :, s0:s0 + 512])
            dxk = dma(o, xk_sb[:, :].rearrange("p (c s) -> p c s", c=NCH),
                      xkd.ap().rearrange("(c p) s -> p c s", p=128)[:, :, s0:s0 + 512])
            wait(o, "tensor", "d", dxk)
            kk_evs = [0] * NCH
            kact_evs = [0] * NCH
            for d in range(NCH):
                ps = banks[d % 4]
                if ti > 0 and d < 4:
                    wait(o, "tensor", "v", gt_last_v)
                if d >= 4:
                    wait(o, "tensor", "v", kk_evs[d - 4])
                for c in range(NCH):
                    fn = lambda e, c2=c, d2=d, p2=ps: e.matmul(
                        p2[:, :], wk_sb[:, c2 * C + 128 * d2: c2 * C + 128 * (d2 + 1)],
                        xk_sb[:, 512 * c2:512 * (c2 + 1)],
                        start=(c2 == 0), stop=(c2 == NCH - 1))
                    if c == NCH - 1:
                        mmk = ev(o, "tensor", fn)
                    else:
                        op(o, "tensor", fn)
                wait(o, "vector", "t", mmk)
                pk = d & 1
                if d >= 2:
                    wait(o, "vector", "s", kact_evs[d - 2])
                kk_evs[d] = ev(o, "vector", lambda e, p2=ps, pk2=pk: e.tensor_scalar_max(
                    kkb[:, pk2 * 512:(pk2 + 1) * 512], p2[:, :], 0.0))
                wait(o, "scalar", "v", kk_evs[d])
                kact_evs[d] = ev(o, "scalar", lambda e, d2=d, pk2=pk: e.activation(
                    kact[:, 512 * d2:512 * (d2 + 1)], kkb[:, pk2 * 512:(pk2 + 1) * 512],
                    AF.Square))
            wait(o, "tensor", "d", dxr)
            r_evs = [0] * 8
            for u in range(4):
                for hf in range(2):
                    idx = 2 * u + hf
                    ps = banks[idx % 4]
                    if idx < 4:
                        wait(o, "tensor", "v", kk_evs[4 + idx])
                    else:
                        wait(o, "tensor", "s", r_evs[idx - 4])
                    for c in range(NCH):
                        fn = lambda e, c2=c, u2=u, h2=hf, p2=ps: e.matmul(
                            p2[:, :], xr_sb[:, 512 * c2 + 128 * u2: 512 * c2 + 128 * (u2 + 1)],
                            wr_sb[:, c2 * C + 512 * h2: c2 * C + 512 * (h2 + 1)],
                            start=(c2 == 0), stop=(c2 == NCH - 1))
                        if c == NCH - 1:
                            mmr = ev(o, "tensor", fn)
                        else:
                            op(o, "tensor", fn)
                    wait(o, "scalar", "t", mmr)
                    r_evs[idx] = ev(o, "scalar", lambda e, u2=u, h2=hf, p2=ps: e.activation(
                        r_sb[:, C * u2 + 512 * h2: C * u2 + 512 * (h2 + 1)],
                        p2[:, :], AF.Sigmoid, bias=0.0, scale=1.0))
            v_evs = [0] * 8
            for u in range(4):
                pu = 0
                g = ti * 4 + u
                if g >= 1:
                    wait(o, "sync", "v", res_prev[g - 1])
                dx2u = dma(o, x2t[:, :],
                           x2d.ap()[s0 + 128 * u:s0 + 128 * (u + 1), :])
                for hf in range(2):
                    idx = 2 * u + hf
                    ps = banks[idx % 4]
                    if idx == 0:
                        wait(o, "tensor", "s", kact_evs[NCH - 1])
                    if idx < 4:
                        wait(o, "tensor", "s", r_evs[4 + idx])
                    else:
                        wait(o, "tensor", "v", v_evs[idx - 4])
                    for d in range(NCH):
                        fn = lambda e, d2=d, u2=u, h2=hf, p2=ps: e.matmul(
                            p2[:, :], kact[:, 512 * d2 + 128 * u2: 512 * d2 + 128 * (u2 + 1)],
                            wv_sb[:, d2 * C + 512 * h2: d2 * C + 512 * (h2 + 1)],
                            start=(d2 == 0), stop=(d2 == NCH - 1))
                        if d == NCH - 1:
                            mmv = ev(o, "tensor", fn)
                        else:
                            op(o, "tensor", fn)
                    wait(o, "vector", "t", mmv)
                    if g > 0 or hf > 0:
                        wait(o, "vector", "s", ln4_sq_prev)
                    v_evs[idx] = ev(o, "vector", lambda e, u2=u, h2=hf, p2=ps: e.tensor_mul(
                        ffn[:, C * u2 + 512 * h2: C * u2 + 512 * (h2 + 1)],
                        p2[:, :], r_sb[:, C * u2 + 512 * h2: C * u2 + 512 * (h2 + 1)]))
                    v_mm_last = mmv
                gt_last_v = v_evs[2 * u + 1]
                pp = g & 1
                fsl = ffn[:, C * u:C * (u + 1)]
                v1 = ev(o, "vector", lambda e, a=fsl, p2=pp: e.reduce_sum(SX(p2), a, AX.X))
                wait(o, "scalar", "v", v1)
                ln4_sq_prev = ev(o, "scalar", lambda e, a=fsl, p2=pp: e.activation(
                    junk[:, :], a, AF.Square, accum_out=SQ(p2)))
                wait(o, "vector", "s", ln4_sq_prev)
                v3 = small_chain(o, pp)
                wait(o, "scalar", "v", v3)
                if g >= 2:
                    wait(o, "scalar", "v", res_prev[g - 2])
                s3 = ev(o, "scalar", lambda e, a=fsl, p2=pp: e.activation(
                    tmp[:, p2 * C:(p2 + 1) * C], a, AF.Identity,
                    bias=NBIAS(p2), scale=RSTD(p2)))
                wait(o, "vector", "s", s3)
                wait(o, "vector", "d", dx2u)
                if g >= 1:
                    wait(o, "vector", "d", out_prev[g - 1])
                res_prev[g] = ev(o, "vector", lambda e, pu2=pu, p2=pp: e.tensor_add(
                    outb[:, :], tmp[:, p2 * C:(p2 + 1) * C],
                    x2t[:, pu2 * C:(pu2 + 1) * C]))
                wait(o, "sync", "v", res_prev[g])
                out_prev[g] = dma(
                    o, out_h.ap()[s0 + 128 * u:s0 + 128 * (u + 1), :],
                    outb[:, :])
        wait(o, "sync", "d", ct["d"])
        wait(o, "sync", "g", ct["g"])
        o.emit()

    return nc


_NC_CACHE = {}


def _get_nc(debug=False):
    if debug not in _NC_CACHE:
        _NC_CACHE[debug] = build_nc(debug)
    return _NC_CACHE[debug]


def kernel(L0_attn_query, L0_attn_conv_weight, L0_ffn_W_r, L0_ffn_W_k,
           L0_ffn_W_v, L0_ffn_mu_r, L0_ffn_mu_k):
    x = np.ascontiguousarray(np.asarray(L0_attn_query, dtype=np.float32))
    cw = np.asarray(L0_attn_conv_weight, dtype=np.float32)
    wr = np.asarray(L0_ffn_W_r, dtype=np.float32)
    wk = np.asarray(L0_ffn_W_k, dtype=np.float32)
    wv = np.asarray(L0_ffn_W_v, dtype=np.float32)
    mur = np.asarray(L0_ffn_mu_r, dtype=np.float32)
    muk = np.asarray(L0_ffn_mu_k, dtype=np.float32)

    wrT = np.ascontiguousarray(wr.T)
    wkT = np.ascontiguousarray(wk.T)
    wvT = np.ascontiguousarray(wv.T)
    cwq = np.zeros((NCH, 128, 162), dtype=np.float32)
    cwr = cw.reshape(H, 9, HD, K9)
    for cc in range(NCH):
        for hl in range(2):
            h = 2 * cc + hl
            blk = np.transpose(cwr[h], (1, 2, 0))
            cols = (np.arange(K9)[:, None] * 18 + hl * 9 + np.arange(9)[None, :])
            cwq[cc, 64 * hl:64 * (hl + 1), cols.reshape(-1)] = blk.reshape(HD, 81).T
    mur8 = np.ascontiguousarray(mur.reshape(NCH, 128).T)
    muk8 = np.ascontiguousarray(muk.reshape(NCH, 128).T)
    idb = np.eye(128, dtype=np.float32).astype(ml_dtypes.bfloat16)

    nc = _get_nc()
    in_maps = []
    for b in range(NCORES):
        in_maps.append({
            "xin": np.ascontiguousarray(x[b]),
            "wrT": wrT, "wkT": wkT, "wvT": wvT,
            "cwq": cwq, "mur": mur8, "muk": muk8, "idbh": idb,
        })
    res = run_bass_kernel_spmd(nc, in_maps, core_ids=list(range(NCORES)))
    out = np.stack([np.asarray(r["out"]) for r in res.results], axis=0)
    return out.astype(np.float32)


# version 31
# speedup vs baseline: 1.3011x; 1.1539x over previous
import contextlib
import numpy as np
import ml_dtypes
import concourse.bass as bass
import concourse.mybir as mybir
from concourse.bass_utils import run_bass_kernel_spmd

F32 = mybir.dt.float32
BF16 = mybir.dt.bfloat16
AF = mybir.ActivationFunctionType
ALU = mybir.AluOpType
AX = mybir.AxisListType

S = 4096
C = 1024
NB = S // 128
NCH = 8
H = 16
HD = 64
K9 = 9
EPS = 1e-5
NCORES = 8


def build_nc(debug=False):
    nc = bass.Bass("TRN2", target_bir_lowering=False)

    xin = nc.dram_tensor("xin", [S, C], F32, kind="ExternalInput")
    wrT_h = nc.dram_tensor("wrT", [C, C], F32, kind="ExternalInput")
    wkT_h = nc.dram_tensor("wkT", [C, C], F32, kind="ExternalInput")
    wvT_h = nc.dram_tensor("wvT", [C, C], F32, kind="ExternalInput")
    cwq_h = nc.dram_tensor("cwq", [NCH, 128, 162], F32, kind="ExternalInput")
    mur_h = nc.dram_tensor("mur", [128, NCH], F32, kind="ExternalInput")
    muk_h = nc.dram_tensor("muk", [128, NCH], F32, kind="ExternalInput")
    idb_h = nc.dram_tensor("idbh", [128, 128], BF16, kind="ExternalInput")
    out_h = nc.dram_tensor("out", [S, C], F32, kind="ExternalOutput")

    kd = "ExternalOutput" if debug else "Internal"
    ln1tm = nc.dram_tensor("ln1tm", [S + 8, C], BF16, kind=kd)
    ln1fm = nc.dram_tensor("ln1fm", [C, S], BF16, kind=kd)
    qdr = nc.dram_tensor("qdr", [S + 8, 9 * 144], F32, kind=kd)
    wtd = nc.dram_tensor("wtd", [S, 144], BF16, kind=kd)
    x2d = nc.dram_tensor("x2d", [S, C], F32, kind=kd)
    ln3fm = nc.dram_tensor("ln3fm", [C, S + 1], BF16, kind=kd)
    xrd = nc.dram_tensor("xrd", [C, S], BF16, kind=kd)
    xkd = nc.dram_tensor("xkd", [C, S], BF16, kind=kd)
    tmpdbg = nc.dram_tensor("tmpdbg", [S, C], F32, kind=kd)
    ydbg = nc.dram_tensor("ydbg", [S, C], F32, kind=kd)
    svdbg = nc.dram_tensor("svdbg", [S, 20], F32, kind=kd)

    with contextlib.ExitStack() as st:
        def sb(name, shape, dt=F32):
            return st.enter_context(nc.sbuf_tensor(name, shape, dt))

        wr_sb = sb("wr_sb", [128, NCH * C], BF16)
        wk_sb = sb("wk_sb", [128, NCH * C], BF16)
        wv_sb = sb("wv_sb", [128, NCH * C], BF16)
        cwq_sb = sb("cwq_sb", [128, NCH * 162], BF16)
        mur_sb = sb("mur_sb", [128, NCH])
        muk_sb = sb("muk_sb", [128, NCH])
        idb = sb("idb", [128, 128], BF16)
        consts = sb("consts", [128, 4])

        xbuf = sb("xbuf", [128, 2 * C])
        lnb = sb("lnb", [128, 2 * C], BF16)
        fmst = sb("fmst", [128, 2 * C], BF16)
        junk = sb("junk", [128, C])
        sv = sb("sv", [128, 20])
        fmwin = sb("fmwin", [128, 2 * NCH * 128], BF16)
        qstage = sb("qstage", [128, 2 * 1296])
        acc = sb("acc", [128, 2 * 144])
        mx = sb("mx", [128, 2 * H])
        zsum = sb("zsum", [128, 2 * H])
        wtb = sb("wtb", [128, 2 * 144], BF16)
        xk9 = sb("xk9", [128, 9 * C], BF16)
        acc2 = sb("acc2", [128, C])
        ptmp = sb("ptmp", [128, C])
        acc2d = sb("acc2d", [128, C])
        tmp = sb("tmp", [128, 2 * C])
        x2b = sb("x2b", [128, 2 * C])
        fmc = sb("fmc", [128, 2 * (C + 1)], BF16)
        diff = sb("diff", [128, C], BF16)
        xrb = sb("xrb", [128, 2 * C], BF16)
        xkb = sb("xkb", [128, 2 * C], BF16)
        xr_sb = sb("xr_sb", [128, NCH * 512], BF16)
        xk_sb = sb("xk_sb", [128, NCH * 512], BF16)
        kact = sb("kact", [128, NCH * 512], BF16)
        kkb = sb("kkb", [128, 2 * 512])
        r_sb = sb("r_sb", [128, 4 * C], BF16)
        ffn = sb("ffn", [128, 4 * C])
        x2t = sb("x2t", [128, C])
        outb = sb("outb", [128, C])
        dum = sb("dum", [128, 4])

        psQa = st.enter_context(nc.psum_tensor("psQa", [128, 162], F32))
        psQb = st.enter_context(nc.psum_tensor("psQb", [128, 162], F32))
        psT0 = st.enter_context(nc.psum_tensor("psT0", [128, C], BF16))
        psT1 = st.enter_context(nc.psum_tensor("psT1", [128, C], BF16))
        psA = st.enter_context(nc.psum_tensor("psA", [128, 512], F32))
        psB = st.enter_context(nc.psum_tensor("psB", [128, 512], F32))
        psC = st.enter_context(nc.psum_tensor("psC", [128, 512], F32))
        psD = st.enter_context(nc.psum_tensor("psD", [128, 512], F32))

        dsm = st.enter_context(nc.semaphore("dsm"))
        gsm = st.enter_context(nc.semaphore("gsm"))
        vsm = st.enter_context(nc.semaphore("vsm"))
        ssm = st.enter_context(nc.semaphore("ssm"))
        tsm = st.enter_context(nc.semaphore("tsm"))
        psm = st.enter_context(nc.semaphore("psm"))

        nc.const_aps.register(F32, 0.0, consts[:, 0:1])
        nc.const_aps.register(F32, 1.0, consts[:, 1:2])
        nc.const_aps.register(F32, float(EPS), consts[:, 2:3])

        ct = dict(d=0, g=0, v=0, s=0, t=0, p=0)
        SEMS = dict(d=dsm, g=gsm, v=vsm, s=ssm, t=tsm, p=psm)
        EKEY = dict(vector="v", scalar="s", tensor="t", gpsimd="p")

        class Ops:
            def __init__(self):
                self.l = dict(sync=[], vector=[], scalar=[], tensor=[], gpsimd=[])

            def add(self, eng, fn):
                self.l[eng].append(fn)

            def emit(self):
                with nc.Block() as blk:
                    for eng, dec in (("sync", blk.sync), ("vector", blk.vector),
                                     ("scalar", blk.scalar), ("tensor", blk.tensor),
                                     ("gpsimd", blk.gpsimd)):
                        if self.l[eng]:
                            dec(lambda e, L=self.l[eng]: [f(e) for f in L])

        def dma(ops, dst, src, eng="sync", **kw):
            key = "g" if eng == "gpsimd" else "d"
            sem = SEMS[key]
            ct[key] += 16
            ops.add(eng, lambda e, a=dst, b=src, s2=sem, k2=kw:
                    e.dma_start(a, b, **k2).then_inc(s2, 16))
            return ct[key]

        def wait(ops, eng, sem_key, val):
            if val:
                ops.add(eng, lambda e, s2=SEMS[sem_key], v2=val: e.wait_ge(s2, v2))

        def ev(ops, eng, fn):
            key = EKEY[eng]
            sem = SEMS[key]
            ct[key] += 1
            ops.add(eng, lambda e, f=fn, s2=sem: f(e).then_inc(s2, 1))
            return ct[key]

        def op(ops, eng, fn):
            ops.add(eng, fn)

        def SX(p): return sv[:, 10 * p + 0: 10 * p + 1]
        def SQ(p): return sv[:, 10 * p + 1: 10 * p + 2]
        def MEAN(p): return sv[:, 10 * p + 2: 10 * p + 3]
        def NMEAN(p): return sv[:, 10 * p + 3: 10 * p + 4]
        def VAR(p): return sv[:, 10 * p + 4: 10 * p + 5]
        def RSTD(p): return sv[:, 10 * p + 5: 10 * p + 6]
        def NBIAS(p): return sv[:, 10 * p + 6: 10 * p + 7]
        def TMPV(p): return sv[:, 10 * p + 7: 10 * p + 8]
        def SXB(p): return sv[:, 10 * p + 8: 10 * p + 9]
        def SQB(p): return sv[:, 10 * p + 9: 10 * p + 10]

        def small_chain(o, p, extra_banks=False):
            if extra_banks:
                op(o, "vector", lambda e: e.tensor_add(SX(p), SX(p), SXB(p)))
                op(o, "vector", lambda e: e.tensor_add(SQ(p), SQ(p), SQB(p)))
                op(o, "vector", lambda e: e.drain())
            op(o, "vector", lambda e: e.tensor_scalar_mul(MEAN(p), SX(p), 1.0 / C))
            op(o, "vector", lambda e: e.tensor_scalar_mul(NMEAN(p), SX(p), -1.0 / C))
            op(o, "vector", lambda e: e.drain())
            op(o, "vector", lambda e: e.tensor_mul(TMPV(p), MEAN(p), MEAN(p)))
            op(o, "vector", lambda e: e.tensor_scalar_mul(VAR(p), SQ(p), 1.0 / C))
            op(o, "vector", lambda e: e.drain())
            op(o, "vector", lambda e: e.tensor_sub(VAR(p), VAR(p), TMPV(p)))
            v2 = ev(o, "vector", lambda e: e.tensor_copy(dum[:, 0:1], consts[:, 0:1]))
            wait(o, "scalar", "v", v2)
            op(o, "scalar", lambda e: e.activation(
                RSTD(p), VAR(p), AF.Sqrt, bias=float(EPS), scale=1.0))
            s2 = ev(o, "scalar", lambda e: e.copy(dum[:, 2:3], consts[:, 0:1]))
            wait(o, "vector", "s", s2)
            op(o, "vector", lambda e: e.reciprocal(RSTD(p), RSTD(p)))
            op(o, "vector", lambda e: e.drain())
            op(o, "vector", lambda e: e.tensor_mul(NBIAS(p), NMEAN(p), RSTD(p)))
            v3 = ev(o, "vector", lambda e: e.tensor_copy(dum[:, 0:1], consts[:, 0:1]))
            return v3

        o = Ops()
        op(o, "vector", lambda e: e.memset(consts[:, :], 0.0))
        op(o, "vector", lambda e: e.drain())
        op(o, "vector", lambda e: e.memset(consts[:, 1:2], 1.0))
        op(o, "vector", lambda e: e.drain())
        op(o, "vector", lambda e: e.memset(consts[:, 2:3], float(EPS)))
        op(o, "vector", lambda e: e.drain())
        op(o, "vector", lambda e: e.memset(qstage[:, :], 0.0))
        zdone = ev(o, "vector", lambda e: e.memset(lnb[:, :], 0.0))
        dma(o, idb[:, :], idb_h.ap())
        wait(o, "sync", "v", zdone)
        dma(o, ln1tm.ap()[0:4, :], lnb[0:4, 0:C])
        dma(o, ln1tm.ap()[S + 4:S + 8, :], lnb[0:4, 0:C])
        dma(o, qdr.ap()[0:4, :], qstage[0:4, 0:1296])
        pad_done = dma(o, qdr.ap()[S + 4:S + 8, :], qstage[0:4, 0:1296])
        dma(o, ln3fm.ap().rearrange("(c p) s -> p c s", p=128)[:, :, 0:1],
            lnb[:, 0:NCH][:, :, None], allow_slow_non_contiguous=True)
        cast_done = 0
        for wsrc, wdst in ((wrT_h, wr_sb), (wkT_h, wk_sb), (wvT_h, wv_sb)):
            for c in range(NCH):
                if cast_done:
                    wait(o, "sync", "g", cast_done)
                dl = dma(o, junk[:, :], wsrc.ap()[128 * c:128 * (c + 1), :])
                wait(o, "gpsimd", "d", dl)
                cast_done = dma(o, wdst[:, c * C:(c + 1) * C], junk[:, :], eng="gpsimd")
        wait(o, "sync", "d", pad_done)
        dl = dma(o, qstage[:, 0:NCH * 162].rearrange("p (c n) -> p c n", c=NCH),
                 cwq_h.ap().rearrange("c p n -> p c n"))
        wait(o, "gpsimd", "d", dl)
        dma(o, cwq_sb[:, :], qstage[:, 0:NCH * 162], eng="gpsimd")
        dma(o, mur_sb[:, :], mur_h.ap())
        dma(o, muk_sb[:, :], muk_h.ap())
        o.emit()

        def ln_phase(src_h, tm_dst, fm_dst, fm_col0):
            o = Ops()
            dx = [0] * NB
            sq = [0] * NB
            red = [0] * NB
            apl = [0] * NB
            tr = [0] * NB
            evc = [0] * NB
            tms = [0] * NB
            fms = [0] * NB
            v3a = [0] * NB

            def stage_l(i):
                p = i & 1
                if i >= 2:
                    wait(o, "sync", "s", apl[i - 2])
                dx[i] = dma(o, xbuf[:, p * C:(p + 1) * C],
                            src_h.ap()[128 * i:128 * (i + 1), :])

            def stage_a(i):
                p = i & 1
                wait(o, "vector", "d", dx[i])
                red[i] = ev(o, "vector", lambda e, p2=p: e.reduce_sum(
                    SX(p2), xbuf[:, p2 * C:(p2 + 1) * C], AX.X))
                wait(o, "scalar", "d", dx[i])
                wait(o, "scalar", "v", red[i])
                if i >= 2:
                    wait(o, "scalar", "v", v3a[i - 2])
                op(o, "scalar", lambda e, p2=p: e.activation(
                    junk[:, :], xbuf[:, p2 * C:(p2 + 1) * C], AF.Square,
                    accum_out=SQ(p2)))
                sq[i] = ev(o, "scalar", lambda e: e.copy(dum[:, 2:3], consts[:, 0:1]))

            def stage_b(i):
                p = i & 1
                wait(o, "vector", "s", sq[i])
                v3 = small_chain(o, p)
                v3a[i] = v3
                wait(o, "scalar", "v", v3)
                if i >= 2:
                    wait(o, "scalar", "t", tr[i - 2])
                    if tm_dst is not None:
                        wait(o, "scalar", "d", tms[i - 2])
                op(o, "scalar", lambda e, p2=p: e.activation(
                    lnb[:, p2 * C:(p2 + 1) * C], xbuf[:, p2 * C:(p2 + 1) * C],
                    AF.Identity, bias=NBIAS(p2), scale=RSTD(p2)))
                apl[i] = ev(o, "scalar", lambda e: e.copy(dum[:, 2:3], consts[:, 0:1]))
                if tm_dst is not None:
                    wait(o, "sync", "s", apl[i])
                    tms[i] = dma(o, tm_dst.ap()[4 + 128 * i:4 + 128 * (i + 1), :],
                                 lnb[:, p * C:(p + 1) * C])
                wait(o, "tensor", "s", apl[i])
                if i >= 2:
                    wait(o, "tensor", "v", evc[i - 2])
                pst = psT0 if p == 0 else psT1
                for c in range(NCH):
                    fn = lambda e, c2=c, p2=p, ps=pst: e.transpose(
                        ps[:, 128 * c2:128 * (c2 + 1)],
                        lnb[:, p2 * C + 128 * c2:p2 * C + 128 * (c2 + 1)], idb[:, :])
                    if c == NCH - 1:
                        tr[i] = ev(o, "tensor", fn)
                    else:
                        op(o, "tensor", fn)
                wait(o, "vector", "t", tr[i])
                if i >= 2:
                    wait(o, "vector", "d", fms[i - 2])
                op(o, "vector", lambda e, p2=p, ps=pst: e.tensor_copy(
                    fmst[:, p2 * C:(p2 + 1) * C], ps[:, :]))
                evc[i] = ev(o, "vector", lambda e: e.tensor_copy(dum[:, 0:1], consts[:, 0:1]))
                wait(o, "sync", "v", evc[i])
                fms[i] = dma(
                    o, fm_dst.ap().rearrange("(c p) s -> p c s", p=128)
                    [:, :, fm_col0 + 128 * i:fm_col0 + 128 * (i + 1)],
                    fmst[:, p * C:(p + 1) * C].rearrange("p (c s) -> p c s", c=NCH))

            for i in range(NB + 2):
                if i >= 2:
                    stage_b(i - 2)
                if i < NB:
                    stage_l(i)
                if 1 <= i <= NB:
                    stage_a(i - 1)
            o.emit()

        ln_phase(xin, ln1tm, ln1fm, 0)

        o = Ops()
        mml = {}
        evq = {}
        qst = [0] * NB
        for i in range(NB):
            p = i & 1
            if i >= 2:
                wait(o, "sync", "t", mml[(i - 2, NCH - 1)])
            dw = dma(o, fmwin[:, p * NCH * 128:(p + 1) * NCH * 128]
                     .rearrange("p (c s) -> p c s", c=NCH),
                     ln1fm.ap().rearrange("(c p) s -> p c s", p=128)
                     [:, :, 128 * i:128 * (i + 1)])
            wait(o, "tensor", "d", dw)
            for c in range(NCH):
                ps = psQa if (c % 2 == 0) else psQb
                if i > 0 or c >= 2:
                    prev = evq[(i, c - 2)] if c >= 2 else evq[(i - 1, NCH - 2 + c)]
                    wait(o, "tensor", "s", prev)
                mml[(i, c)] = ev(o, "tensor", lambda e, c2=c, p2=p, ps2=ps: e.matmul(
                    ps2[:, :],
                    fmwin[:, p2 * NCH * 128 + 128 * c2:p2 * NCH * 128 + 128 * (c2 + 1)],
                    cwq_sb[:, 162 * c2:162 * (c2 + 1)], start=True, stop=True))
                wait(o, "scalar", "t", mml[(i, c)])
                if c == 0 and i >= 2:
                    wait(o, "scalar", "d", qst[i - 2])
                evq[(i, c)] = ev(o, "scalar", lambda e, c2=c, p2=p, ps2=ps: e.activation(
                    qstage[:, p2 * 1296:(p2 + 1) * 1296]
                    .rearrange("p (t h j) -> p t h j", t=9, h=H)[:, :, 2 * c2:2 * c2 + 2, :],
                    ps2[:, :].rearrange("p (t hl j) -> p t hl j", t=9, hl=2),
                    AF.Identity, bias=0.0, scale=1.0))
            wait(o, "sync", "s", evq[(i, NCH - 1)])
            qst[i] = dma(o, qdr.ap()[4 + 128 * i:4 + 128 * (i + 1), :],
                         qstage[:, p * 1296:(p + 1) * 1296])
        o.emit()

        o = Ops()
        addv = [0] * NB
        sm_end = [0] * NB
        wts = [0] * NB

        dq4 = [0] * NB

        def p4_l(i):
            p = i & 1
            if i >= 2:
                wait(o, "sync", "p", addv[i - 2])
            for t in range(K9):
                dq4[i] = dma(o, qstage[:, p * 1296 + 144 * t:p * 1296 + 144 * (t + 1)],
                             qdr.ap()[128 * i + t:128 * i + t + 128, 144 * t:144 * (t + 1)])

        def p4_a(i):
            p = i & 1
            wait(o, "gpsimd", "d", dq4[i])
            if i >= 2:
                wait(o, "gpsimd", "v", sm_end[i - 2])
            q0 = p * 1296
            op(o, "gpsimd", lambda e, p2=p, q2=q0: e.tensor_add(
                acc[:, p2 * 144:(p2 + 1) * 144], qstage[:, q2:q2 + 144],
                qstage[:, q2 + 144:q2 + 288]))
            for t in range(2, K9):
                op(o, "gpsimd", lambda e, p2=p, q2=q0, t2=t: e.tensor_add(
                    acc[:, p2 * 144:(p2 + 1) * 144], acc[:, p2 * 144:(p2 + 1) * 144],
                    qstage[:, q2 + 144 * t2:q2 + 144 * (t2 + 1)]))
            addv[i] = ev(o, "gpsimd", lambda e: e.tensor_copy(dum[:, 1:2], consts[:, 0:1]))

        def p4_b(i):
            p = i & 1
            av = acc[:, p * 144:(p + 1) * 144].rearrange("p (h j) -> p h j", h=H)
            wait(o, "vector", "p", addv[i])
            op(o, "vector", lambda e, a=av, p2=p: e.reduce_max(
                mx[:, p2 * H:(p2 + 1) * H][:, :, None], a, AX.X))
            op(o, "vector", lambda e: e.drain())
            v1 = ev(o, "vector", lambda e, a=av, p2=p: e.tensor_sub(
                a, a, mx[:, p2 * H:(p2 + 1) * H][:, :, None].broadcast_to([128, H, 9])))
            wait(o, "scalar", "v", v1)
            op(o, "scalar", lambda e, p2=p: e.activation(
                acc[:, p2 * 144:(p2 + 1) * 144], acc[:, p2 * 144:(p2 + 1) * 144],
                AF.Exp, bias=0.0, scale=1.0))
            se = ev(o, "scalar", lambda e: e.copy(dum[:, 2:3], consts[:, 0:1]))
            wait(o, "vector", "s", se)
            op(o, "vector", lambda e, a=av, p2=p: e.reduce_sum(
                zsum[:, p2 * H:(p2 + 1) * H][:, :, None], a, AX.X))
            op(o, "vector", lambda e: e.drain())
            op(o, "vector", lambda e, p2=p: e.reciprocal(
                zsum[:, p2 * H:(p2 + 1) * H], zsum[:, p2 * H:(p2 + 1) * H]))
            op(o, "vector", lambda e: e.drain())
            if i >= 2:
                wait(o, "vector", "d", wts[i - 2])
            op(o, "vector", lambda e, a=av, p2=p: e.tensor_mul(
                wtb[:, p2 * 144:(p2 + 1) * 144].rearrange("p (h j) -> p h j", h=H),
                a, zsum[:, p2 * H:(p2 + 1) * H][:, :, None].broadcast_to([128, H, 9])))
            sm_end[i] = ev(o, "vector", lambda e: e.tensor_copy(dum[:, 0:1], consts[:, 0:1]))
            wait(o, "sync", "v", sm_end[i])
            wts[i] = dma(o, wtd.ap()[128 * i:128 * (i + 1), :],
                         wtb[:, p * 144:(p + 1) * 144])

        for i in range(NB + 2):
            if i >= 2:
                p4_b(i - 2)
            if i < NB:
                p4_l(i)
            if 1 <= i <= NB:
                p4_a(i - 1)
        o.emit()

        o = Ops()
        mulev = {}
        mul_done = [0] * NB
        apl6 = [0] * NB
        res6 = [0] * NB
        x2s = [0] * NB
        dwt6 = [0] * NB
        dxb6 = [0] * NB
        dk6 = {}
        v1a = [0] * NB

        def p6_l(i):
            p = i & 1
            if i >= 2:
                wait(o, "sync", "v", res6[i - 2])
            dwt6[i] = dma(o, wtb[:, p * 144:(p + 1) * 144],
                          wtd.ap()[128 * i:128 * (i + 1), :])
            dxb6[i] = dma(o, xbuf[:, p * C:(p + 1) * C],
                          xin.ap()[128 * i:128 * (i + 1), :])
            for k in range(K9):
                qeng = "sync" if k % 2 == 0 else "gpsimd"
                if i > 0:
                    wait(o, qeng, "v", mulev[(i - 1, k)])
                dk6[(i, k)] = dma(o, xk9[:, C * k:C * (k + 1)],
                                  ln1tm.ap()[128 * i + k:128 * i + k + 128, :],
                                  eng=qeng)

        def p6_a(i):
            p = i & 1
            acc1 = x2b[:, p * C:(p + 1) * C]
            for k in range(K9):
                src = xk9[:, C * k:C * (k + 1)].rearrange("p (h d) -> p h d", h=H)
                wsl = wtb[:, p * 144:(p + 1) * 144].rearrange(
                    "p (h j) -> p h j", h=H)[:, :, k:k + 1]
                wait(o, "vector", "d" if k % 2 == 0 else "g", dk6[(i, k)])
                if k == 0:
                    wait(o, "vector", "d", dwt6[i])
                    if i >= 2:
                        wait(o, "vector", "d", x2s[i - 2])
                    dst = acc1.rearrange("p (h d) -> p h d", h=H)
                    mulev[(i, k)] = ev(o, "vector", lambda e, a=src, b=wsl, d2=dst:
                                       e.tensor_mul(d2, a, b.broadcast_to([128, H, HD])))
                else:
                    if k == 1 and i >= 2:
                        wait(o, "vector", "s", apl6[i - 2])
                    dst = tmp[:, p * C:(p + 1) * C].rearrange("p (h d) -> p h d", h=H)
                    mulev[(i, k)] = ev(o, "vector", lambda e, a=src, b=wsl, d2=dst:
                                       e.tensor_mul(d2, a, b.broadcast_to([128, H, HD])))
                    fn = lambda e, a1=acc1, p2=p: e.tensor_add(
                        a1, a1, tmp[:, p2 * C:(p2 + 1) * C])
                    if k == K9 - 1:
                        mul_done[i] = ev(o, "vector", fn)
                    else:
                        op(o, "vector", fn)
            op(o, "vector", lambda e, a1=acc1, p2=p: e.reduce_sum(SX(p2), a1, AX.X))
            v1a[i] = ev(o, "vector", lambda e: e.tensor_copy(dum[:, 0:1], consts[:, 0:1]))

        def p6_b(i):
            p = i & 1
            acc1 = x2b[:, p * C:(p + 1) * C]
            wait(o, "scalar", "v", v1a[i])
            op(o, "scalar", lambda e, a1=acc1, p2=p: e.activation(
                junk[:, :], a1, AF.Square, accum_out=SQ(p2)))
            sqe = ev(o, "scalar", lambda e: e.copy(dum[:, 2:3], consts[:, 0:1]))
            wait(o, "vector", "s", sqe)
            v3 = small_chain(o, p)
            wait(o, "scalar", "v", v3)
            if i >= 2:
                wait(o, "scalar", "v", res6[i - 2])
            op(o, "scalar", lambda e, a1=acc1, p2=p: e.activation(
                tmp[:, p2 * C:(p2 + 1) * C], a1, AF.Identity,
                bias=NBIAS(p2), scale=RSTD(p2)))
            apl6[i] = ev(o, "scalar", lambda e: e.copy(dum[:, 2:3], consts[:, 0:1]))
            wait(o, "vector", "s", apl6[i])
            wait(o, "vector", "d", dxb6[i])
            op(o, "vector", lambda e, a1=acc1, p2=p: e.tensor_add(
                a1, tmp[:, p2 * C:(p2 + 1) * C], xbuf[:, p2 * C:(p2 + 1) * C]))
            res6[i] = ev(o, "vector", lambda e: e.tensor_copy(dum[:, 0:1], consts[:, 0:1]))
            wait(o, "sync", "v", res6[i])
            x2s[i] = dma(o, x2d.ap()[128 * i:128 * (i + 1), :], acc1)

        p6_l(0)
        for i in range(1, NB + 2):
            if i >= 2:
                p6_b(i - 2)
            if i - 1 < NB:
                p6_a(i - 1)
            if i < NB:
                p6_l(i)
        o.emit()

        ln_phase(x2d, None, ln3fm, 1)

        o = Ops()
        NIT = NCH * 4
        xr_ev = [0] * NIT
        xk_ev = [0] * NIT
        st_xr = [0] * NIT
        st_xk = [0] * NIT
        dcv = [0] * NIT

        def p9_l(it):
            c, hh = divmod(it, 4)
            p = it & 1
            if it >= 2:
                wait(o, "sync", "v", xk_ev[it - 2])
            dcv[it] = dma(o, fmc[:, p * (C + 1):(p + 1) * (C + 1)],
                          ln3fm.ap().rearrange("(c p) s -> p c s", p=128)
                          [:, c, C * hh:C * hh + C + 1])

        def p9_c(it):
            c, hh = divmod(it, 4)
            p = it & 1
            wait(o, "vector", "d", dcv[it])
            op(o, "vector", lambda e, p2=p: e.tensor_sub(
                diff[:, :], fmc[:, p2 * (C + 1) + 1:(p2 + 1) * (C + 1)],
                fmc[:, p2 * (C + 1):p2 * (C + 1) + C]))
            if it >= 2:
                wait(o, "vector", "d", st_xr[it - 2])
            op(o, "vector", lambda e, c2=c, p2=p: e.scalar_tensor_tensor(
                xrb[:, p2 * C:(p2 + 1) * C], diff[:, :], mur_sb[:, c2:c2 + 1],
                fmc[:, p2 * (C + 1):p2 * (C + 1) + C], ALU.mult, ALU.add))
            xr_ev[it] = ev(o, "vector", lambda e: e.tensor_copy(dum[:, 0:1], consts[:, 0:1]))
            if it >= 2:
                wait(o, "vector", "d", st_xk[it - 2])
            op(o, "vector", lambda e, c2=c, p2=p: e.scalar_tensor_tensor(
                xkb[:, p2 * C:(p2 + 1) * C], diff[:, :], muk_sb[:, c2:c2 + 1],
                fmc[:, p2 * (C + 1):p2 * (C + 1) + C], ALU.mult, ALU.add))
            xk_ev[it] = ev(o, "vector", lambda e: e.tensor_copy(dum[:, 0:1], consts[:, 0:1]))
            wait(o, "sync", "v", xr_ev[it])
            st_xr[it] = dma(o, xrd.ap()[128 * c:128 * (c + 1), C * hh:C * (hh + 1)],
                            xrb[:, p * C:(p + 1) * C])
            wait(o, "sync", "v", xk_ev[it])
            st_xk[it] = dma(o, xkd.ap()[128 * c:128 * (c + 1), C * hh:C * (hh + 1)],
                            xkb[:, p * C:(p + 1) * C])

        p9_l(0)
        for it in range(1, NIT + 1):
            if it - 1 < NIT:
                p9_c(it - 1)
            if it < NIT:
                p9_l(it)
        o.emit()

        o = Ops()
        banks = (psA, psB, psC, psD)
        res_prev = [0] * 32
        out_prev = [0] * 32
        gt_last_v = 0
        v_mm_last = 0
        ln4_sq_prev = 0
        for ti in range(8):
            s0 = 512 * ti
            if ti > 0:
                wait(o, "sync", "t", v_mm_last)
                wait(o, "sync", "v", gt_last_v)
            dxr = dma(o, xr_sb[:, :].rearrange("p (c s) -> p c s", c=NCH),
                      xrd.ap().rearrange("(c p) s -> p c s", p=128)[:, :, s0:s0 + 512])
            dxk = dma(o, xk_sb[:, :].rearrange("p (c s) -> p c s", c=NCH),
                      xkd.ap().rearrange("(c p) s -> p c s", p=128)[:, :, s0:s0 + 512])
            wait(o, "tensor", "d", dxk)
            kk_evs = [0] * NCH
            kact_evs = [0] * NCH
            for d in range(NCH):
                ps = banks[d % 4]
                if ti > 0 and d < 4:
                    wait(o, "tensor", "v", gt_last_v)
                if d >= 4:
                    wait(o, "tensor", "v", kk_evs[d - 4])
                for c in range(NCH):
                    fn = lambda e, c2=c, d2=d, p2=ps: e.matmul(
                        p2[:, :], wk_sb[:, c2 * C + 128 * d2: c2 * C + 128 * (d2 + 1)],
                        xk_sb[:, 512 * c2:512 * (c2 + 1)],
                        start=(c2 == 0), stop=(c2 == NCH - 1))
                    if c == NCH - 1:
                        mmk = ev(o, "tensor", fn)
                    else:
                        op(o, "tensor", fn)
                wait(o, "vector", "t", mmk)
                pk = d & 1
                if d >= 2:
                    wait(o, "vector", "s", kact_evs[d - 2])
                op(o, "vector", lambda e, p2=ps, pk2=pk: e.tensor_scalar_max(
                    kkb[:, pk2 * 512:(pk2 + 1) * 512], p2[:, :], 0.0))
                kk_evs[d] = ev(o, "vector", lambda e: e.tensor_copy(dum[:, 0:1], consts[:, 0:1]))
                wait(o, "scalar", "v", kk_evs[d])
                op(o, "scalar", lambda e, d2=d, pk2=pk: e.activation(
                    kact[:, 512 * d2:512 * (d2 + 1)], kkb[:, pk2 * 512:(pk2 + 1) * 512],
                    AF.Square))
                kact_evs[d] = ev(o, "scalar", lambda e: e.copy(dum[:, 2:3], consts[:, 0:1]))
            wait(o, "tensor", "d", dxr)
            r_evs = [0] * 8
            for u in range(4):
                for hf in range(2):
                    idx = 2 * u + hf
                    ps = banks[idx % 4]
                    if idx < 4:
                        wait(o, "tensor", "v", kk_evs[4 + idx])
                    else:
                        wait(o, "tensor", "s", r_evs[idx - 4])
                    for c in range(NCH):
                        fn = lambda e, c2=c, u2=u, h2=hf, p2=ps: e.matmul(
                            p2[:, :], xr_sb[:, 512 * c2 + 128 * u2: 512 * c2 + 128 * (u2 + 1)],
                            wr_sb[:, c2 * C + 512 * h2: c2 * C + 512 * (h2 + 1)],
                            start=(c2 == 0), stop=(c2 == NCH - 1))
                        if c == NCH - 1:
                            mmr = ev(o, "tensor", fn)
                        else:
                            op(o, "tensor", fn)
                    wait(o, "scalar", "t", mmr)
                    op(o, "scalar", lambda e, u2=u, h2=hf, p2=ps: e.activation(
                        r_sb[:, C * u2 + 512 * h2: C * u2 + 512 * (h2 + 1)],
                        p2[:, :], AF.Sigmoid, bias=0.0, scale=1.0))
                    r_evs[idx] = ev(o, "scalar", lambda e: e.copy(dum[:, 2:3], consts[:, 0:1]))
            v_evs = [0] * 8
            for u in range(4):
                pu = 0
                g = ti * 4 + u
                if g >= 1:
                    wait(o, "sync", "v", res_prev[g - 1])
                dx2u = dma(o, x2t[:, :],
                           x2d.ap()[s0 + 128 * u:s0 + 128 * (u + 1), :])
                for hf in range(2):
                    idx = 2 * u + hf
                    ps = banks[idx % 4]
                    if idx == 0:
                        wait(o, "tensor", "s", kact_evs[NCH - 1])
                    if idx < 4:
                        wait(o, "tensor", "s", r_evs[4 + idx])
                    else:
                        wait(o, "tensor", "v", v_evs[idx - 4])
                    for d in range(NCH):
                        fn = lambda e, d2=d, u2=u, h2=hf, p2=ps: e.matmul(
                            p2[:, :], kact[:, 512 * d2 + 128 * u2: 512 * d2 + 128 * (u2 + 1)],
                            wv_sb[:, d2 * C + 512 * h2: d2 * C + 512 * (h2 + 1)],
                            start=(d2 == 0), stop=(d2 == NCH - 1))
                        if d == NCH - 1:
                            mmv = ev(o, "tensor", fn)
                        else:
                            op(o, "tensor", fn)
                    wait(o, "vector", "t", mmv)
                    if g > 0 or hf > 0:
                        wait(o, "vector", "s", ln4_sq_prev)
                    op(o, "vector", lambda e, u2=u, h2=hf, p2=ps: e.tensor_mul(
                        ffn[:, C * u2 + 512 * h2: C * u2 + 512 * (h2 + 1)],
                        p2[:, :], r_sb[:, C * u2 + 512 * h2: C * u2 + 512 * (h2 + 1)]))
                    v_evs[idx] = ev(o, "vector", lambda e: e.tensor_copy(dum[:, 0:1], consts[:, 0:1]))
                    v_mm_last = mmv
                gt_last_v = v_evs[2 * u + 1]
                pp = g & 1
                fsl = ffn[:, C * u:C * (u + 1)]
                v1 = ev(o, "vector", lambda e, a=fsl, p2=pp: e.reduce_sum(SX(p2), a, AX.X))
                wait(o, "scalar", "v", v1)
                op(o, "scalar", lambda e, a=fsl, p2=pp: e.activation(
                    junk[:, :], a, AF.Square, accum_out=SQ(p2)))
                ln4_sq_prev = ev(o, "scalar", lambda e: e.copy(dum[:, 2:3], consts[:, 0:1]))
                wait(o, "vector", "s", ln4_sq_prev)
                v3 = small_chain(o, pp)
                wait(o, "scalar", "v", v3)
                if g >= 2:
                    wait(o, "scalar", "v", res_prev[g - 2])
                op(o, "scalar", lambda e, a=fsl, p2=pp: e.activation(
                    tmp[:, p2 * C:(p2 + 1) * C], a, AF.Identity,
                    bias=NBIAS(p2), scale=RSTD(p2)))
                s3 = ev(o, "scalar", lambda e: e.copy(dum[:, 2:3], consts[:, 0:1]))
                wait(o, "vector", "s", s3)
                wait(o, "vector", "d", dx2u)
                if g >= 1:
                    wait(o, "vector", "d", out_prev[g - 1])
                op(o, "vector", lambda e, pu2=pu, p2=pp: e.tensor_add(
                    outb[:, :], tmp[:, p2 * C:(p2 + 1) * C],
                    x2t[:, pu2 * C:(pu2 + 1) * C]))
                res_prev[g] = ev(o, "vector", lambda e: e.tensor_copy(dum[:, 0:1], consts[:, 0:1]))
                wait(o, "sync", "v", res_prev[g])
                out_prev[g] = dma(
                    o, out_h.ap()[s0 + 128 * u:s0 + 128 * (u + 1), :],
                    outb[:, :])
        wait(o, "sync", "d", ct["d"])
        wait(o, "sync", "g", ct["g"])
        o.emit()

    return nc


_NC_CACHE = {}


def _get_nc(debug=False):
    if debug not in _NC_CACHE:
        _NC_CACHE[debug] = build_nc(debug)
    return _NC_CACHE[debug]


def kernel(L0_attn_query, L0_attn_conv_weight, L0_ffn_W_r, L0_ffn_W_k,
           L0_ffn_W_v, L0_ffn_mu_r, L0_ffn_mu_k):
    x = np.ascontiguousarray(np.asarray(L0_attn_query, dtype=np.float32))
    cw = np.asarray(L0_attn_conv_weight, dtype=np.float32)
    wr = np.asarray(L0_ffn_W_r, dtype=np.float32)
    wk = np.asarray(L0_ffn_W_k, dtype=np.float32)
    wv = np.asarray(L0_ffn_W_v, dtype=np.float32)
    mur = np.asarray(L0_ffn_mu_r, dtype=np.float32)
    muk = np.asarray(L0_ffn_mu_k, dtype=np.float32)

    wrT = np.ascontiguousarray(wr.T)
    wkT = np.ascontiguousarray(wk.T)
    wvT = np.ascontiguousarray(wv.T)
    cwq = np.zeros((NCH, 128, 162), dtype=np.float32)
    cwr = cw.reshape(H, 9, HD, K9)
    for cc in range(NCH):
        for hl in range(2):
            h = 2 * cc + hl
            blk = np.transpose(cwr[h], (1, 2, 0))
            cols = (np.arange(K9)[:, None] * 18 + hl * 9 + np.arange(9)[None, :])
            cwq[cc, 64 * hl:64 * (hl + 1), cols.reshape(-1)] = blk.reshape(HD, 81).T
    mur8 = np.ascontiguousarray(mur.reshape(NCH, 128).T)
    muk8 = np.ascontiguousarray(muk.reshape(NCH, 128).T)
    idb = np.eye(128, dtype=np.float32).astype(ml_dtypes.bfloat16)

    nc = _get_nc()
    in_maps = []
    for b in range(NCORES):
        in_maps.append({
            "xin": np.ascontiguousarray(x[b]),
            "wrT": wrT, "wkT": wkT, "wvT": wvT,
            "cwq": cwq, "mur": mur8, "muk": muk8, "idbh": idb,
        })
    res = run_bass_kernel_spmd(nc, in_maps, core_ids=list(range(NCORES)))
    out = np.stack([np.asarray(r["out"]) for r in res.results], axis=0)
    return out.astype(np.float32)
